# version 02
# baseline (speedup 1.0000x reference)
import os
import numpy as np
import ml_dtypes
from contextlib import ExitStack

import concourse.bass as bass
import concourse.mybir as mybir
from concourse.bass_utils import run_bass_kernel_spmd

F32 = mybir.dt.float32
BF16 = mybir.dt.bfloat16
U8 = mybir.dt.uint8
AF = mybir.ActivationFunctionType
ALU = mybir.AluOpType
BF = ml_dtypes.bfloat16

D = 4096
I = 11008
H = 32
L = 2048
OWN = 1024
NG_D = 32
NG_I = 86
NS2 = 43
NBLK = 8
EPS = 1e-5
QSCL = float(1.0 / np.sqrt(128.0))
NEG = -1.0e9

ENGINES = ("sync", "tensor", "vector", "scalar", "gpsimd")


def _pi(n):
    p = np.arange(n)
    blk, k = p // 128, p % 128
    return np.where(k < 64, blk * 128 + 2 * k, blk * 128 + 2 * (k - 64) + 1)


PI_D = _pi(D)
PI_I = _pi(I)


class Sem:
    def __init__(self, ctx, nc, name):
        self.h = ctx.enter_context(nc.semaphore(name))
        self.n = 0


SERIAL = bool(int(os.environ.get("KSERIAL", "1")))


class Prog:
    def __init__(self):
        self.ops = {e: [] for e in ENGINES}
        self.all_sems = []
        self.eng_sem = {}

    def _serial_wait(self):
        return [self.last] if getattr(self, "last", None) else []

    def emit(self, eng, fn):
        self.ops[eng].append(fn)

    def dma(self, q, out, in_, sem, wait=()):
        w = tuple(self._serial_wait()) + tuple(wait) if SERIAL else tuple(wait)
        if SERIAL:
            self.last = (sem, sem.n + 16)

        def f(e, out=out, in_=in_, w=w, sem=sem):
            for (s, v) in w:
                e.wait_ge(s.h, v)
            e.dma_start(out=out, in_=in_).then_inc(sem.h, 16)

        self.emit(q, f)
        sem.n += 16
        return sem.n

    def op(self, eng, fn, sem=None, inc=1, wait=()):
        if SERIAL:
            wait = tuple(self._serial_wait()) + tuple(wait)
            if sem is None:
                sem = self.eng_sem[eng]
            self.last = (sem, sem.n + inc)
        w = tuple(wait)

        def f(e, fn=fn, w=w, sem=sem, inc=inc):
            for (s, vv) in w:
                e.wait_ge(s.h, vv)
            ins = fn(e)
            if sem is not None:
                ins.then_inc(sem.h, inc)

        self.emit(eng, f)
        if sem is not None:
            sem.n += inc
            return sem.n
        return None

    def waits(self, eng, wait):
        w = tuple(wait)

        def f(e, w=w):
            for (s, v) in w:
                e.wait_ge(s.h, v)

        self.emit(eng, f)

    def barrier(self, sems):
        w = [(s, s.n) for s in sems]
        for e in ENGINES:
            self.waits(e, w)


def r3(ap, k):
    return ap.rearrange("p (a b) -> p a b", b=k)


def build_program(dump=None):
    nc = bass.Bass("TRN2", target_bir_lowering=False)

    din = lambda n, s, t: nc.dram_tensor(n, s, t, kind="ExternalInput").ap()
    xT = din("xT", [D, L], F32)
    qgi = din("qg", [128, NS2, NG_D, 128], U8)
    qui = din("qu", [128, NS2, NG_D, 128], U8)
    sgi = din("sg", [NG_D, I], F32)
    sui = din("su", [NG_D, I], F32)
    zgq = din("zgq", [NG_D, I // 2], U8)
    zuq = din("zuq", [NG_D, I // 2], U8)
    qdi = din("qd", [128, NG_I, D // 2], U8)
    sdi = din("sd", [NG_I, D], F32)
    zdq = din("zdq", [NG_I, D // 2], U8)
    wqi = din("wq", [128, 32, NG_D, 128], F32)
    wki = din("wk", [128, 32, NG_D, 128], F32)
    woi = din("wo", [128, 32, 32, 128], F32)
    wvi = din("wv", [128, NG_D, D], F32)
    bqi = din("bq", [128, 32], F32)
    bki = din("bk", [128, 32], F32)
    bvi = din("bv", [128, 32], F32)
    boi = din("bo", [128, 32], F32)
    qposi = din("qpos", [128, NBLK], F32)
    kposi = din("kpos", [1, L], F32)

    oT = nc.dram_tensor("oT", [D, OWN], F32, kind="ExternalOutput").ap()
    dbg = None
    if dump == "xmid":
        dbg = nc.dram_tensor("dbg", [D, L], F32, kind="ExternalOutput").ap()
    elif dump == "xmid2":
        dbg = nc.dram_tensor("dbg", [D, OWN], F32, kind="ExternalOutput").ap()
    elif dump == "kv":
        dbg = nc.dram_tensor("dbg", [2 * D, L], F32, kind="ExternalOutput").ap()
    elif dump == "ln0":
        dbg = nc.dram_tensor("dbg", [128, NG_D * 512], BF16, kind="ExternalOutput").ap()
        dbg2 = nc.dram_tensor("dbg2", [NG_D, 512], BF16, kind="ExternalOutput").ap()
        dbg3 = nc.dram_tensor("dbg3", [1, 2048], F32, kind="ExternalOutput").ap()
    elif dump == "h0":
        dbg = nc.dram_tensor("dbg", [128, NG_I * 512], BF16, kind="ExternalOutput").ap()
        dbg2 = nc.dram_tensor("dbg2", [NG_I, 512], BF16, kind="ExternalOutput").ap()
    elif dump == "od":
        dbg = nc.dram_tensor("dbg", [128, 8 * 4 * 512], BF16, kind="ExternalOutput").ap()
    elif dump == "dq":
        dbg = nc.dram_tensor("dbg", [128, 8, NG_D, 128], BF16, kind="ExternalOutput").ap()
        dbg2 = nc.dram_tensor("dbg2", [128, 2, D], BF16, kind="ExternalOutput").ap()
        dbg3 = nc.dram_tensor("dbg3", [NG_D, 256], BF16, kind="ExternalOutput").ap()

    cg = nc.dram_tensor("cg", [128, NG_I, NG_D, 128], BF16, kind="Internal").ap()
    cu = nc.dram_tensor("cu", [128, NG_I, NG_D, 128], BF16, kind="Internal").ap()
    cd = nc.dram_tensor("cd", [128, NG_I, D], BF16, kind="Internal").ap()
    zs_g = nc.dram_tensor("zs_g", [NG_D, I], BF16, kind="Internal").ap()
    zs_u = nc.dram_tensor("zs_u", [NG_D, I], BF16, kind="Internal").ap()
    zs_d = nc.dram_tensor("zs_d", [NG_I, D], BF16, kind="Internal").ap()
    xmid = nc.dram_tensor("xmid", [D, L], F32, kind="Internal").ap()
    xmid2 = nc.dram_tensor("xmid2", [D, OWN], F32, kind="Internal").ap()
    kTd = nc.dram_tensor("kTd", [D, L], BF16, kind="Internal").ap()
    vd = nc.dram_tensor("vd", [L, D], BF16, kind="Internal").ap()

    P = Prog()

    with ExitStack() as ctx:
        sb = lambda n, s, t: ctx.enter_context(nc.sbuf_tensor(n, s, t))

        s_w = Sem(ctx, nc, "s_w")
        s_io = Sem(ctx, nc, "s_io")
        s_sc = Sem(ctx, nc, "s_sc")
        s_pe = Sem(ctx, nc, "s_pe")
        s_dv = Sem(ctx, nc, "s_dv")
        s_ac = Sem(ctx, nc, "s_ac")
        s_gp = Sem(ctx, nc, "s_gp")
        P.all_sems = [s_w, s_io, s_sc, s_pe, s_dv, s_ac, s_gp]
        P.eng_sem = {"tensor": s_pe, "vector": s_dv, "scalar": s_ac,
                     "gpsimd": s_gp, "sync": s_w}

        Ap = sb("poolA", [128, 48 * 1024], U8)
        Bp = sb("poolB", [128, 64 * 1024], U8)
        Cp = sb("poolC", [128, 87 * 1024], U8)
        ones_f = sb("ones_f", [1, 128], F32)
        ones_b = sb("ones_b", [128, 1], BF16)
        iota_p = sb("iota_p", [128, 1], F32)
        iota256 = sb("iota256", [128, 256], F32)
        ident_b = sb("ident_b", [128, 128], BF16)
        bq_t = sb("bq_t", [128, 32], F32)
        bk_t = sb("bk_t", [128, 32], F32)
        bv_t = sb("bv_t", [128, 32], F32)
        bo_t = sb("bo_t", [128, 32], F32)
        qpos_t = sb("qpos_t", [128, NBLK], F32)
        masks = sb("masks", [128, NBLK * 256], BF16)
        cvals = sb("cvals", [1, 8], F32)
        CONSTS = (EPS, 1.0 / D, -0.5, QSCL, 1.0, 0.5)
        for _i, _v in enumerate(CONSTS):
            nc.const_aps.register(F32, _v, cvals[0:1, _i:_i + 1])
        gs32 = sb("gs32", [128, 2 * NG_D - 1], BF16)
        gs86 = sb("gs86", [128, 2 * NG_I - 1], BF16)

        fen_v = sb("fen_v", [1, 2], F32)
        fen_a = sb("fen_a", [1, 2], F32)

        def fence_v(wait=()):
            return P.op("vector", lambda e: e.memset(fen_v[0:1, :], 0.0), sem=s_dv,
                        wait=wait)

        def fence_a(wait=()):
            return P.op("scalar", lambda e: e.memzero(fen_a[0:1, :]), sem=s_ac,
                        wait=wait)

        PSA = ctx.enter_context(nc.psum_tensor("PSA", [128, 2048], F32))
        PSB = ctx.enter_context(nc.psum_tensor("PSB", [128, 2048], F32))
        a_ = [PSA[:, i * 512:(i + 1) * 512] for i in range(4)]
        b_ = [PSB[:, i * 512:(i + 1) * 512] for i in range(4)]

        def cv(pool, off, cols, dt):
            return pool[:, off:off + cols * mybir.dt.size(dt)].bitcast(dt)

        def g32sl(g):
            return gs32[:, NG_D - 1 - g:2 * NG_D - 1 - g]

        def g86sl(n):
            return gs86[:, NG_I - 1 - n:2 * NG_I - 1 - n]

        def f_cvals(e):
            last = None
            for _i, _v in enumerate(CONSTS):
                last = e.memset(cvals[0:1, _i:_i + 1], _v)
            return last
        P.op("vector", f_cvals, sem=s_dv)

        def f_iota(e):
            e.iota(iota_p[:], pattern=[[0, 1]], base=0, channel_multiplier=1,
                   allow_small_or_imprecise_dtypes=True)
            return e.iota(iota256[:], pattern=[[1, 256]], base=0, channel_multiplier=0,
                          allow_small_or_imprecise_dtypes=True)
        P.op("gpsimd", f_iota, sem=s_gp)

        kpos_t = cv(Bp, 56 * 1024, L, F32)[0:1, :]
        for t, srci in ((bq_t[:], bqi), (bk_t[:], bki), (bv_t[:], bvi), (bo_t[:], boi),
                        (qpos_t[:], qposi), (kpos_t, kposi)):
            P.dma("sync", t, srci, s_w)
        v_bias = s_w.n

        def f_const(e):
            e.memset(ones_f[:], 1.0)
            e.memset(ones_b[:], 1.0)
            e.memset(gs32[:], 0.0)
            e.memset(gs86[:], 0.0)
            e.memset(gs32[:, NG_D - 1:NG_D], -1.0)
            e.memset(gs86[:, NG_I - 1:NG_I], -1.0)
            return e.tensor_scalar(out=ident_b[:], in0=iota256[:, 0:128],
                                   scalar1=iota_p[:, 0:1], scalar2=None, op0=ALU.is_equal)
        P.op("vector", f_const, sem=s_dv, wait=[(s_gp, 1)])

        def f_krep(e):
            last = None
            for j in range(NBLK):
                last = e.matmul(PSA[:, j * 256:(j + 1) * 256], ones_f[:],
                                kpos_t[0:1, 2 * j * 128:2 * j * 128 + 256],
                                start=True, stop=True)
            return last
        vkr = P.op("tensor", f_krep, sem=s_pe, wait=[(s_w, v_bias), (s_dv, 1)])

        def f_masks(e):
            last = None
            for j in range(NBLK):
                last = e.tensor_scalar(
                    out=masks[:, j * 256:(j + 1) * 256],
                    in0=PSA[:, j * 256:(j + 1) * 256],
                    scalar1=qpos_t[:, j:j + 1], scalar2=None, op0=ALU.is_gt)
            return last
        P.op("vector", f_masks, sem=s_dv, wait=[(s_pe, vkr)])

        def f_bq(e):
            return e.tensor_scalar(out=bq_t[:], in0=bq_t[:], scalar1=QSCL, scalar2=None,
                                   op0=ALU.mult)
        P.op("vector", f_bq, sem=s_dv, wait=[(s_w, v_bias)])

        zrow = cv(Ap, 0, I // 2, U8)
        zlo = cv(Ap, 8 * 1024, I // 2, U8)
        zhi = cv(Ap, 16 * 1024, I // 2, U8)
        sful = cv(Bp, 0, I, F32)
        zout = cv(Cp, 0, I, BF16)

        def zbuild(qz, s_ap, zdram, npart, ncol):
            half = ncol // 2
            guard = [(s_dv, s_dv.n)]
            v0 = P.dma("gpsimd", zrow[0:npart, 0:half], qz, s_io, wait=guard)
            v1 = P.dma("sync", sful[0:npart, 0:ncol], s_ap, s_w, wait=guard)

            def f(e):
                e.tensor_scalar(out=zlo[0:npart, 0:half], in0=zrow[0:npart, 0:half],
                                scalar1=15, scalar2=None, op0=ALU.bitwise_and)
                e.tensor_scalar(out=zhi[0:npart, 0:half], in0=zrow[0:npart, 0:half],
                                scalar1=4, scalar2=None, op0=ALU.logical_shift_right)
                z3 = r3(zout[0:npart, 0:ncol], 128)
                s3 = r3(sful[0:npart, 0:ncol], 128)
                l3 = r3(zlo[0:npart, 0:half], 64)
                h3 = r3(zhi[0:npart, 0:half], 64)
                e.tensor_tensor(out=z3[:, :, 0:64], in0=l3, in1=s3[:, :, 0:64], op=ALU.mult)
                return e.tensor_tensor(out=z3[:, :, 64:128], in0=h3, in1=s3[:, :, 64:128],
                                       op=ALU.mult)
            P.op("vector", f, sem=s_dv, wait=[(s_io, v0), (s_w, v1)])
            vv = fence_v()
            return P.dma("gpsimd", zdram, zout[0:npart, 0:ncol], s_io, wait=[(s_dv, vv)])

        zbuild(zgq, sgi, zs_g, NG_D, I)
        zbuild(zuq, sui, zs_u, NG_D, I)
        zbuild(zdq, sdi, zs_d, NG_I, D)
        P.barrier([s_io, s_dv, s_w, s_gp, s_pe])

        cw_gate = [0] * NG_I
        cw_up = [0] * NG_I
        cw_down = [0] * NG_I
        qb2 = [cv(Ap, 0, 4096, U8), cv(Ap, 4 * 1024, 4096, U8)]
        ulo = cv(Ap, 8 * 1024, 4096, U8)
        uhi = cv(Ap, 12 * 1024, 4096, U8)
        dqs = [cv(Bp, 0, 32 * 256, BF16), cv(Bp, 16 * 1024, 32 * 256, BF16),
               cv(Bp, 32 * 1024, 32 * 256, BF16), cv(Bp, 48 * 1024, 32 * 256, BF16)]
        srw = [cv(Ap, 16 * 1024, 256, F32)[0:1, :], cv(Ap, 17 * 1024, 256, F32)[0:1, :]]

        def dq_matrix(qsrc, ssrc, cdst, cwlist, s0, s1, qbufs, slabbufs):
            dv_after = [0] * NS2
            for s in range(NS2):
                bi = s % 2
                vq = P.dma("gpsimd", qbufs[bi][:], qsrc[:, s], s_io,
                           wait=[(s_dv, dv_after[s - 2])] if s >= 2 else [])

                def fum(e, qb=qbufs[bi]):
                    e.tensor_scalar(out=ulo[:], in0=qb[:], scalar1=15, scalar2=None,
                                    op0=ALU.bitwise_and)
                    return e.tensor_scalar(out=uhi[:], in0=qb[:], scalar1=4,
                                           scalar2=None, op0=ALU.logical_shift_right)
                vum = P.op("vector", fum, sem=s_dv, wait=[(s_io, vq)])

                for g in range(NG_D):
                    sbi = g % 2
                    vsr = P.dma("sync", srw[sbi][0:1, :],
                                ssrc[g:g + 1, s * 256:(s + 1) * 256], s_sc,
                                wait=[(s_pe, s_pe.n)])

                    def frp(e, sbi=sbi, s0=s0, s1=s1):
                        return e.matmul((s0 if sbi == 0 else s1)[:, 0:256], ones_f[:],
                                        srw[sbi][0:1, :], start=True, stop=True)
                    vrp = P.op("tensor", frp, sem=s_pe, wait=[(s_sc, vsr), (s_dv, s_dv.n)])

                    def fsc(e, g=g, sbi=sbi, s0=s0, s1=s1, slab=slabbufs[bi]):
                        o4 = r3(slab[:], 32 * 128).rearrange("p a (g k) -> p a g k", k=128)
                        l3 = r3(ulo[:, g * 128:(g + 1) * 128], 64)
                        h3 = r3(uhi[:, g * 128:(g + 1) * 128], 64)
                        sr3 = r3((s0 if sbi == 0 else s1)[:, 0:256], 128)
                        e.tensor_tensor(out=o4[:, :, g, 0:64], in0=l3, in1=sr3[:, :, 0:64],
                                        op=ALU.mult)
                        return e.tensor_tensor(out=o4[:, :, g, 64:128], in0=h3,
                                               in1=sr3[:, :, 64:128], op=ALU.mult)
                    wfsc = [(s_pe, vrp), (s_dv, vum)]
                    if g == 0 and s >= 2:
                        wfsc.append((s_io, cwlist[2 * (s - 2)]))
                    P.op("vector", fsc, sem=s_dv, wait=wfsc)
                dv_after[s] = fence_v()
                vcw = P.dma("gpsimd", cdst[:, 2 * s:2 * s + 2],
                            r3(slabbufs[bi][:], 32 * 128), s_io, wait=[(s_dv, dv_after[s])])
                cwlist[2 * s] = vcw
                cwlist[2 * s + 1] = vcw

        dq_matrix(qgi, sgi, cg, cw_gate, b_[0], b_[1], qb2, dqs[0:2])
        dq_matrix(qui, sui, cu, cw_up, b_[0], b_[1], qb2[::-1], dqs[2:4])

        qrow = [cv(Ap, 0, 2048, U8), cv(Ap, 2 * 1024, 2048, U8)]
        dlo = cv(Ap, 4 * 1024, 2048, U8)
        dhi = cv(Ap, 6 * 1024, 2048, U8)
        dout = [cv(Ap, 8 * 1024, D, BF16), cv(Ap, 16 * 1024, D, BF16)]
        srd = [cv(Bp, 0, D, F32)[0:1, :], cv(Bp, 16 * 1024, D, F32)[0:1, :]]
        dv_aft = [0] * NG_I
        v_gu_done = [(s_io, s_io.n), (s_dv, s_dv.n)]
        for g in range(NG_I):
            bi = g % 2
            vs = P.dma("sync", srd[bi][0:1, :], sdi[g:g + 1, :], s_sc,
                       wait=[(s_pe, s_pe.n)] + ([] if g >= 2 else v_gu_done))
            vq = P.dma("gpsimd", qrow[bi][:], qdi[:, g, :], s_io,
                       wait=[(s_dv, dv_aft[g - 2])] if g >= 2 else v_gu_done)

            def fum(e, bi=bi):
                e.tensor_scalar(out=dlo[:], in0=qrow[bi][:], scalar1=15, scalar2=None,
                                op0=ALU.bitwise_and)
                return e.tensor_scalar(out=dhi[:], in0=qrow[bi][:], scalar1=4,
                                       scalar2=None, op0=ALU.logical_shift_right)
            vum = P.op("vector", fum, sem=s_dv, wait=[(s_io, vq)])

            for c in range(4):
                def frp(e, bi=bi, c=c):
                    e.matmul(b_[0][:], ones_f[:], srd[bi][0:1, c * 1024:c * 1024 + 512],
                             start=True, stop=True)
                    return e.matmul(b_[1][:], ones_f[:],
                                    srd[bi][0:1, c * 1024 + 512:(c + 1) * 1024],
                                    start=True, stop=True)
                vrp = P.op("tensor", frp, sem=s_pe, wait=[(s_sc, vs), (s_dv, s_dv.n)])

                def ftt(e, bi=bi, c=c):
                    o3 = r3(dout[bi][:, c * 1024:(c + 1) * 1024], 128)
                    l3 = r3(dlo[:, c * 512:(c + 1) * 512], 64)
                    h3 = r3(dhi[:, c * 512:(c + 1) * 512], 64)
                    s0 = r3(b_[0], 128)
                    s1 = r3(b_[1], 128)
                    e.tensor_tensor(out=o3[:, 0:4, 0:64], in0=l3[:, 0:4], in1=s0[:, :, 0:64],
                                    op=ALU.mult)
                    e.tensor_tensor(out=o3[:, 0:4, 64:128], in0=h3[:, 0:4],
                                    in1=s0[:, :, 64:128], op=ALU.mult)
                    e.tensor_tensor(out=o3[:, 4:8, 0:64], in0=l3[:, 4:8], in1=s1[:, :, 0:64],
                                    op=ALU.mult)
                    return e.tensor_tensor(out=o3[:, 4:8, 64:128], in0=h3[:, 4:8],
                                           in1=s1[:, :, 64:128], op=ALU.mult)
                wftt = [(s_pe, vrp), (s_dv, vum)]
                if c == 0 and g >= 2:
                    wftt.append((s_io, cw_down[g - 2]))
                P.op("vector", ftt, sem=s_dv, wait=wftt)
            dv_aft[g] = fence_v()
            cw_down[g] = P.dma("gpsimd", cd[:, g, :], dout[bi][:], s_io,
                               wait=[(s_dv, dv_aft[g])])

        if dump == "dq":
            dqtmp = cv(Bp, 0, 8 * NG_D * 128, BF16)
            v0 = P.dma("gpsimd", dqtmp[:].rearrange("p (a g k) -> p a g k", g=NG_D, k=128),
                       cg[:, 78:86], s_io, wait=[(s_io, s_io.n)])
            P.dma("gpsimd", dbg, dqtmp[:].rearrange("p (a g k) -> p a g k", g=NG_D, k=128),
                  s_io, wait=[(s_io, v0)])
            dqt2 = cv(Bp, 32 * 1024, 2 * D, BF16)
            v0 = P.dma("gpsimd", dqt2[:].rearrange("p (a k) -> p a k", k=D), cd[:, 0:2], s_io)
            P.dma("gpsimd", dbg2, dqt2[:].rearrange("p (a k) -> p a k", k=D), s_io,
                  wait=[(s_io, v0)])
            dqt3 = cv(Bp, 40 * 1024, 256, BF16)
            v0 = P.dma("gpsimd", dqt3[0:NG_D, :], zs_g[:, 0:256], s_io)
            P.dma("gpsimd", dbg3, dqt3[0:NG_D, :], s_io, wait=[(s_io, v0)])
            P.waits("gpsimd", [(s_io, s_io.n), (s_w, s_w.n)])
            _emit(nc, P, ctx)
            return nc

        st_row = cv(Ap, 0, 2048, F32)[0:1, :]
        xck = [cv(Ap, 24 * 1024, 512, F32), cv(Ap, 26 * 1024, 512, F32)]
        x2s = [cv(Ap, 28 * 1024, 512, BF16), cv(Ap, 29 * 1024, 512, BF16)]

        def ln_phase(src, c0, ln_slot, xg_out=None):
            ps_s1 = b_[0][0:1, :]
            ps_s2 = b_[1][0:1, :]
            ps_rep = b_[2]
            ps_rep2 = b_[3]
            ps_xgv = b_[0][0:NG_D, :]
            dvh = [0, 0]
            for g in range(NG_D):
                bi = g % 2
                vx = P.dma("gpsimd", xck[bi][:], src[g * 128:(g + 1) * 128, c0:c0 + 512],
                           s_io, wait=[(s_dv, dvh[bi])])

                def fcv(e, g=g, bi=bi):
                    return e.tensor_copy(out=ln_slot(g), in_=xck[bi][:])
                dvh[bi] = P.op("vector", fcv, sem=s_dv, wait=[(s_io, vx)])

                def fsq(e, g=g, bi=bi):
                    return e.activation(out=x2s[bi][:], in_=ln_slot(g), func=AF.Square)
                vsq = P.op("scalar", fsq, sem=s_ac,
                           wait=[(s_dv, dvh[bi])] + ([(s_pe, s_pe.n)] if g >= 2 else []))

                def fst(e, g=g, bi=bi):
                    e.matmul(ps_s1, ones_b[:], ln_slot(g),
                             start=(g == 0), stop=(g == NG_D - 1))
                    return e.matmul(ps_s2, ones_b[:], x2s[bi][:],
                                    start=(g == 0), stop=(g == NG_D - 1))
                P.op("tensor", fst, sem=s_pe, wait=[(s_ac, vsq)])
            v_stats = s_pe.n

            st_m = st_row[0:1, 0:512]
            st_v = st_row[0:1, 512:1024]
            st_r = st_row[0:1, 1024:1536]
            st_mr = st_row[0:1, 1536:2048]

            def fev(e):
                e.activation(out=st_m, in_=ps_s1, func=AF.Copy, scale=1.0 / D)
                return e.activation(out=st_v, in_=ps_s2, func=AF.Copy, scale=1.0 / D)
            vev = P.op("scalar", fev, sem=s_ac, wait=[(s_pe, v_stats)])

            def fvar(e):
                e.tensor_tensor(out=st_mr, in0=st_m, in1=st_m, op=ALU.mult)
                return e.tensor_tensor(out=st_v, in0=st_v, in1=st_mr, op=ALU.subtract)
            vva = P.op("vector", fvar, sem=s_dv, wait=[(s_ac, vev)])

            def frs(e):
                e.activation(out=st_r, in_=st_v, func=AF.Ln, bias=EPS)
                return e.activation(out=st_r, in_=st_r, func=AF.Exp, scale=-0.5)
            vrs = P.op("scalar", frs, sem=s_ac, wait=[(s_dv, vva)])

            def fmr(e):
                return e.tensor_tensor(out=st_mr, in0=st_m, in1=st_r, op=ALU.mult)
            vmr = P.op("vector", fmr, sem=s_dv, wait=[(s_ac, vrs)])

            def frep(e):
                e.matmul(ps_rep, ones_f[:], st_r, start=True, stop=True)
                return e.matmul(ps_rep2, ones_f[:], st_mr, start=True, stop=True)
            vrep = P.op("tensor", frep, sem=s_pe, wait=[(s_dv, vmr)])

            for g in range(NG_D):
                def fnm(e, g=g):
                    sl = ln_slot(g)
                    e.tensor_tensor(out=sl, in0=sl, in1=ps_rep, op=ALU.mult)
                    return e.tensor_tensor(out=sl, in0=sl, in1=ps_rep2, op=ALU.subtract)
                vn = P.op("vector", fnm, sem=s_dv, wait=[(s_pe, vrep)])
                if xg_out is not None:
                    def fxg(e, g=g):
                        return e.matmul(ps_xgv, g32sl(g), ln_slot(g),
                                        start=(g == 0), stop=(g == NG_D - 1))
                    P.op("tensor", fxg, sem=s_pe, wait=[(s_dv, vn)])
            if xg_out is not None:
                def fxe(e):
                    return e.activation(out=xg_out, in_=ps_xgv, func=AF.Copy)
                P.op("scalar", fxe, sem=s_ac, wait=[(s_pe, s_pe.n)])

        ln_t = cv(Bp, 0, NG_D * 512, BF16)
        lnsl = lambda g: ln_t[:, g * 512:(g + 1) * 512]
        gslab = [cv(Bp, 32 * 1024, 32 * 128, BF16), cv(Bp, 40 * 1024, 32 * 128, BF16)]
        uslab = [cv(Bp, 48 * 1024, 32 * 128, BF16), cv(Bp, 56 * 1024, 32 * 128, BF16)]
        hbuf = cv(Cp, 0, NG_I * 512, BF16)
        xg_t = cv(Ap, 30 * 1024, 512, BF16)
        zscg = [cv(Ap, 31 * 1024, 128, BF16), cv(Ap, 31 * 1024 + 256, 128, BF16)]
        zscu = [cv(Ap, 31 * 1024 + 512, 128, BF16), cv(Ap, 31 * 1024 + 768, 128, BF16)]
        hg_o = cv(Ap, 32 * 1024, 512, BF16)
        sil = [cv(Ap, 33 * 1024, 512, BF16), cv(Ap, 34 * 1024, 512, BF16)]
        wdb = [cv(Ap, 35 * 1024, 512, BF16), cv(Ap, 36 * 1024, 512, BF16)]
        zsdc = cv(Ap, 37 * 1024, 512, BF16)
        obf = cv(Ap, 38 * 1024, 4 * 512, BF16)
        xres = [cv(Ap, 42 * 1024, 512, F32), cv(Ap, 44 * 1024, 512, F32)]
        xoutb = [cv(Ap, 16 * 1024, 512, F32), cv(Ap, 18 * 1024, 512, F32)]

        def gate_up(first_tile):
            ps_gv = [a_[0], a_[1]]
            ps_uv = [a_[2], a_[3]]
            ps_hgv = b_[0][0:NG_I, :]
            ev_g = [0, 0]
            ev_u = [0, 0]
            pe_aft = [0] * NG_I
            for n in range(NG_I):
                bi = n % 2
                wsl = [(s_pe, pe_aft[n - 2])] if n >= 2 else []
                if first_tile:
                    wsl = wsl + [(s_io, cw_gate[n])]
                P.dma("sync", gslab[bi][:], cg[:, n], s_w, wait=wsl)
                wsl2 = [(s_io, cw_up[n])] if first_tile else []
                P.dma("sync", uslab[bi][:], cu[:, n], s_w, wait=wsl2)
                P.dma("sync", zscg[bi][0:NG_D, :], zs_g[:, n * 128:(n + 1) * 128], s_w)
                vz = P.dma("sync", zscu[bi][0:NG_D, :], zs_u[:, n * 128:(n + 1) * 128], s_w)

                def fmm(e, n=n, bi=bi):
                    for g in range(NG_D):
                        e.matmul(ps_gv[bi], gslab[bi][:, g * 128:(g + 1) * 128],
                                 lnsl(g), start=(g == 0), stop=False)
                        e.matmul(ps_uv[bi], uslab[bi][:, g * 128:(g + 1) * 128],
                                 lnsl(g), start=(g == 0), stop=False)
                    e.matmul(ps_gv[bi], zscg[bi][0:NG_D, :], xg_t[0:NG_D, :],
                             start=False, stop=True)
                    return e.matmul(ps_uv[bi], zscu[bi][0:NG_D, :], xg_t[0:NG_D, :],
                                    start=False, stop=True)
                vmm = P.op("tensor", fmm, sem=s_pe,
                           wait=[(s_w, vz), (s_ac, ev_g[bi]), (s_dv, ev_u[bi])])
                pe_aft[n] = vmm

                def fsi(e, bi=bi):
                    return e.activation(out=sil[bi][:], in_=ps_gv[bi], func=AF.Silu)
                ev_g[bi] = P.op("scalar", fsi, sem=s_ac,
                                wait=[(s_pe, vmm), (s_dv, ev_u[bi])])

                def fh(e, n=n, bi=bi):
                    return e.tensor_tensor(out=hbuf[:, n * 512:(n + 1) * 512],
                                           in0=sil[bi][:], in1=ps_uv[bi], op=ALU.mult)
                ev_u[bi] = P.op("vector", fh, sem=s_dv, wait=[(s_ac, ev_g[bi])])

                def fhg(e, n=n):
                    return e.matmul(ps_hgv, g86sl(n),
                                    hbuf[:, n * 512:(n + 1) * 512],
                                    start=(n == 0), stop=(n == NG_I - 1))
                P.op("tensor", fhg, sem=s_pe, wait=[(s_dv, ev_u[bi])])

            def fhe(e):
                return e.activation(out=hg_o[0:NG_I, :], in_=b_[0][0:NG_I, :], func=AF.Copy)
            P.op("scalar", fhe, sem=s_ac, wait=[(s_pe, s_pe.n)])

        psb_bf = PSB[:].bitcast(BF16)

        DOWN_BARRIER = bool(int(os.environ.get("KDOWN_BARRIER", "0")))

        def down(first_tile, x_src, c0s, x_dst, c0d, extra_dst=None, extra_c0=0):
            ps_ov = [a_[0], a_[1], a_[2], a_[3]]
            ev = 0
            vst_l = 0
            io_x = [0, 0]
            pe_aft = [0] * NG_I
            for fs in range(8):
                if DOWN_BARRIER:
                    P.barrier([s_dv, s_ac, s_pe, s_w, s_io, s_sc])
                for g in range(NG_I):
                    bi = g % 2
                    wv_ = [(s_pe, pe_aft[g - 2])]
                    if first_tile and fs == 0:
                        wv_ = wv_ + [(s_io, cw_down[g])]
                    vw = P.dma("sync", wdb[bi][:], cd[:, g, fs * 512:(fs + 1) * 512],
                               s_w, wait=wv_)

                    def fdm(e, g=g, bi=bi):
                        last = None
                        for tc in range(4):
                            last = e.matmul(ps_ov[tc],
                                            hbuf[:, g * 512 + tc * 128:g * 512 + tc * 128 + 128],
                                            wdb[bi][:], start=(g == 0), stop=False)
                        return last
                    pe_aft[g] = P.op("tensor", fdm, sem=s_pe,
                                     wait=[(s_w, vw)] + ([(s_ac, ev)] if g == 0 else []))

                vzd = P.dma("sync", zsdc[0:NG_I, :], zs_d[:, fs * 512:(fs + 1) * 512], s_w,
                            wait=[(s_pe, pe_aft[NG_I - 2])])

                def fcr(e):
                    last = None
                    for tc in range(4):
                        last = e.matmul(ps_ov[tc], hg_o[0:NG_I, tc * 128:(tc + 1) * 128],
                                        zsdc[0:NG_I, :], start=False, stop=True)
                    return last
                vcr = P.op("tensor", fcr, sem=s_pe, wait=[(s_w, vzd)])

                v_tr_prev = s_pe.n
                for tc in range(4):
                    def fev(e, tc=tc):
                        return e.activation(out=obf[:, tc * 512:(tc + 1) * 512],
                                            in_=ps_ov[tc], func=AF.Copy)
                    ev = P.op("scalar", fev, sem=s_ac, wait=[(s_pe, vcr)])
                v_ev = ev
                ev = fence_a()
                if dump == "od" and first_tile and c0d == 0:
                    vfa = fence_a()
                    P.dma("gpsimd", dbg[:, fs * 2048:(fs + 1) * 2048], obf[:], s_io,
                          wait=[(s_ac, vfa)])

                for fsub in range(4):
                    fglob = fs * 4 + fsub
                    bi = fglob % 2

                    def ftr(e, fsub=fsub):
                        last = None
                        for tc in range(4):
                            last = e.transpose(
                                psb_bf[:, tc * 128:(tc + 1) * 128],
                                obf[:, tc * 512 + fsub * 128:tc * 512 + fsub * 128 + 128],
                                ident_b[:])
                        return last
                    vtr = P.op("tensor", ftr, sem=s_pe,
                               wait=[(s_ac, v_ev), (s_dv, vst_l)])

                    vx = P.dma("gpsimd", xres[bi][:],
                               x_src[fglob * 128:(fglob + 1) * 128, c0s:c0s + 512], s_io,
                               wait=[(s_dv, s_dv.n)] if fglob >= 2 else [])

                    def fst(e, bi=bi):
                        return e.scalar_tensor_tensor(
                            out=xoutb[bi][:], in0=psb_bf[:, 0:512],
                            scalar=0.5, in1=xres[bi][:], op0=ALU.mult, op1=ALU.add)
                    P.op("vector", fst, sem=s_dv,
                         wait=[(s_pe, vtr), (s_io, vx), (s_io, io_x[bi])])
                    vst_l = fence_v()
                    io_x[bi] = P.dma("gpsimd",
                                     x_dst[fglob * 128:(fglob + 1) * 128, c0d:c0d + 512],
                                     xoutb[bi][:], s_io, wait=[(s_dv, vst_l)])
                    if extra_dst is not None:
                        io_x[bi] = P.dma(
                            "gpsimd",
                            extra_dst[fglob * 128:(fglob + 1) * 128, extra_c0:extra_c0 + 512],
                            xoutb[bi][:], s_io, wait=[(s_dv, vst_l)])

        for t in range(4):
            ln_phase(xT, t * 512, lnsl, xg_out=xg_t[0:NG_D, :])
            P.barrier([s_dv, s_ac, s_pe, s_io])
            if t == 0 and dump == "ln0":
                P.dma("gpsimd", dbg, ln_t[:], s_io)
                P.dma("gpsimd", dbg2, xg_t[0:NG_D, :], s_io)
                P.dma("gpsimd", dbg3, st_row[0:1, :], s_io)
                P.waits("gpsimd", [(s_io, s_io.n), (s_w, s_w.n)])
                _emit(nc, P, ctx)
                return nc
            gate_up(first_tile=(t == 0))
            P.barrier([s_dv, s_ac, s_pe, s_w, s_io])
            if t == 0 and dump == "h0":
                P.dma("gpsimd", dbg, hbuf[:], s_io)
                P.dma("gpsimd", dbg2, hg_o[0:NG_I, :], s_io)
                P.waits("gpsimd", [(s_io, s_io.n), (s_w, s_w.n)])
                _emit(nc, P, ctx)
                return nc
            down(t == 0, xT, t * 512, xmid, t * 512,
                 extra_dst=dbg if dump == "xmid" else None, extra_c0=t * 512)
            P.barrier([s_dv, s_ac, s_pe, s_w, s_io, s_sc, s_gp])
            if dump == "od" and t == 0:
                P.waits("gpsimd", [(s_io, s_io.n), (s_w, s_w.n)])
                _emit(nc, P, ctx)
                return nc

        if dump == "xmid":
            P.waits("gpsimd", [(s_io, s_io.n), (s_w, s_w.n)])
            _emit(nc, P, ctx)
            return nc

        ln2C = cv(Cp, 0, NG_D * 1024, BF16)
        qres = cv(Bp, 0, 32 * 1024, BF16)
        wslf = cv(Ap, 0, 32 * 128, F32)
        wbfs = [cv(Ap, 16 * 1024, 32 * 128, BF16), cv(Ap, 24 * 1024, 32 * 128, BF16)]
        kob = [cv(Ap, 32 * 1024, 512, BF16), cv(Ap, 33 * 1024, 512, BF16)]
        vvf = [cv(Ap, 34 * 1024, 512, F32), cv(Ap, 36 * 1024, 512, F32)]
        vvb2 = [cv(Ap, 38 * 1024, 512, BF16), cv(Ap, 39 * 1024, 512, BF16)]
        vob = [cv(Ap, 40 * 1024, 512, BF16), cv(Ap, 41 * 1024, 512, BF16)]

        for hf in range(2):
            ln_phase(xmid, hf * 1024, lambda g, hf=hf: ln2C[:, g * 1024:g * 1024 + 512])
            ln_phase(xmid, hf * 1024 + 512,
                     lambda g, hf=hf: ln2C[:, g * 1024 + 512:(g + 1) * 1024])
            P.barrier([s_dv, s_ac, s_pe, s_io, s_w, s_sc])

            io_k = [0, 0]
            for mat, bias_t, isq in ((wki, bk_t, False), (wqi, bq_t, True)):
                for ec in range(32):
                    bi = ec % 2
                    vwf = P.dma("sync", wslf[:], mat[:, ec], s_w, wait=[(s_dv, s_dv.n)])

                    def fcv2(e, bi=bi):
                        return e.tensor_copy(out=wbfs[bi][:], in_=wslf[:])
                    vcv = P.op("vector", fcv2, sem=s_dv,
                               wait=[(s_w, vwf), (s_pe, s_pe.n)])

                    if not isq:
                        for sub in range(2):
                            def fkm(e, bi=bi, sub=sub):
                                last = None
                                for g in range(NG_D):
                                    last = e.matmul(a_[sub],
                                                    wbfs[bi][:, g * 128:(g + 1) * 128],
                                                    ln2C[:, g * 1024 + sub * 512:
                                                         g * 1024 + (sub + 1) * 512],
                                                    start=(g == 0), stop=(g == NG_D - 1))
                                return last
                            vkm = P.op("tensor", fkm, sem=s_pe,
                                       wait=[(s_dv, vcv), (s_ac, s_ac.n)])

                            def fke(e, ec=ec, sub=sub, bias_t=bias_t):
                                return e.activation(out=kob[sub][:], in_=a_[sub],
                                                    func=AF.Identity,
                                                    bias=bias_t[:, ec:ec + 1], scale=1.0)
                            P.op("scalar", fke, sem=s_ac,
                                 wait=[(s_pe, vkm), (s_io, io_k[sub])])
                            vke = fence_a()
                            io_k[sub] = P.dma(
                                "gpsimd",
                                kTd[ec * 128:(ec + 1) * 128,
                                    hf * 1024 + sub * 512:hf * 1024 + (sub + 1) * 512],
                                kob[sub][:], s_ac if False else s_io,
                                wait=[(s_ac, vke)])
                    else:
                        def fqm(e, bi=bi):
                            last = None
                            for g in range(NG_D):
                                rhs = r3(ln2C[:, g * 1024:(g + 1) * 1024], 128)[:, 0:8:2, :]
                                last = e.matmul(a_[2], wbfs[bi][:, g * 128:(g + 1) * 128],
                                                rhs, start=(g == 0), stop=(g == NG_D - 1))
                            return last
                        vqm = P.op("tensor", fqm, sem=s_pe,
                                   wait=[(s_dv, vcv), (s_ac, s_ac.n)])

                        def fqe(e, ec=ec, hf=hf):
                            return e.activation(out=qres[:, ec * 1024 + hf * 512:
                                                        ec * 1024 + (hf + 1) * 512],
                                                in_=a_[2], func=AF.Identity,
                                                bias=bq_t[:, ec:ec + 1], scale=QSCL)
                        P.op("scalar", fqe, sem=s_ac, wait=[(s_pe, vqm)])

            io_vob = [0, 0]
            for eq in range(8):
                vg_after = [0, 0]
                for g in range(NG_D):
                    bi = g % 2
                    vwf = P.dma("sync", vvf[bi][:], wvi[:, g, eq * 512:(eq + 1) * 512],
                                s_w, wait=[(s_pe, vg_after[bi])])

                    def fcv3(e, bi=bi):
                        return e.tensor_copy(out=vvb2[bi][:], in_=vvf[bi][:])
                    vcv = P.op("vector", fcv3, sem=s_dv,
                               wait=[(s_w, vwf), (s_pe, vg_after[bi])])

                    def fvm(e, g=g, bi=bi, hf=hf):
                        last = None
                        for tb in range(8):
                            last = e.matmul(
                                (a_ + b_)[tb],
                                ln2C[:, g * 1024 + tb * 128:g * 1024 + tb * 128 + 128],
                                vvb2[bi][:], start=(g == 0), stop=(g == NG_D - 1))
                        return last
                    vg_after[bi] = P.op("tensor", fvm, sem=s_pe,
                                        wait=[(s_dv, vcv)] +
                                        ([(s_ac, s_ac.n)] if g == 0 else []))
                vvm = s_pe.n
                for tb in range(8):
                    bi2 = tb % 2

                    def fve(e, tb=tb, bi2=bi2):
                        return e.activation(out=vob[bi2][:], in_=(a_ + b_)[tb], func=AF.Copy)
                    P.op("scalar", fve, sem=s_ac,
                         wait=[(s_pe, vvm), (s_io, io_vob[bi2])])
                    vve = fence_a()
                    io_vob[bi2] = P.dma(
                        "gpsimd",
                        vd[hf * 1024 + tb * 128:hf * 1024 + (tb + 1) * 128,
                           eq * 512:(eq + 1) * 512],
                        vob[bi2][:], s_io, wait=[(s_ac, vve)])
            P.barrier([s_dv, s_ac, s_pe, s_io, s_w, s_sc])

        if dump == "kv":
            vtmp = cv(Ap, 0, 2048, BF16)
            vf32 = cv(Ap, 4 * 1024, 2048, F32)
            for g in range(NG_D):
                v0 = P.dma("gpsimd", vtmp[:], kTd[g * 128:(g + 1) * 128, :], s_io)

                def fc(e):
                    return e.tensor_copy(out=vf32[:], in_=vtmp[:])
                vv = P.op("vector", fc, sem=s_dv, wait=[(s_io, v0)])
                P.dma("gpsimd", dbg[g * 128:(g + 1) * 128, :], vf32[:], s_io,
                      wait=[(s_dv, vv)])
                v0 = P.dma("gpsimd", vtmp[:],
                           vd.rearrange("(a p) e -> p (a e)", p=128)[:, g * 2048:0]
                           if False else
                           vd[0:L, g * 128:(g + 1) * 128].rearrange("(a p) e -> p (a e)", p=128),
                           s_io, wait=[(s_dv, vv)])

                def fc2(e):
                    return e.tensor_copy(out=vf32[:], in_=vtmp[:])
                vv = P.op("vector", fc2, sem=s_dv, wait=[(s_io, v0)])
                P.dma("gpsimd", dbg[D + g * 128:D + (g + 1) * 128, :], vf32[:], s_io,
                      wait=[(s_dv, vv)])
            P.waits("gpsimd", [(s_io, s_io.n), (s_w, s_w.n)])
            _emit(nc, P, ctx)
            return nc

        att = cv(Cp, 0, 32 * 1024, BF16)
        khb = [cv(Ap, 0, 2048, BF16), cv(Ap, 4 * 1024, 2048, BF16)]
        vhb = [cv(Ap, 8 * 1024, 2048, BF16), cv(Ap, 12 * 1024, 2048, BF16)]
        scb = [cv(Ap, 16 * 1024, 2048, F32), cv(Ap, 24 * 1024, 2048, F32)]
        pbb = [cv(Ap, 32 * 1024, 2048, BF16), cv(Ap, 36 * 1024, 2048, BF16)]
        ptb = [cv(Ap, 40 * 1024, 2048, BF16), cv(Ap, 44 * 1024, 2048, BF16)]
        nm_t = [sb("nm0", [128, 1], F32), sb("nm1", [128, 1], F32)]
        sum_t = [sb("sum0", [128, 1], F32), sb("sum1", [128, 1], F32)]
        rc_t = [sb("rc0", [128, 1], F32), sb("rc1", [128, 1], F32)]
        ptps = PSB[:].bitcast(BF16)
        avv = PSB[:, 1536:1664]

        items = [(h, j) for h in range(H) for j in range(NBLK)]
        kh_v = [0, 0]
        vh_v = [0, 0]
        g_kh = [0, 0]
        g_vh = [0, 0]
        g_exp = [0, 0]
        g_tr = [0, 0]
        g_av = [0, 0]
        g_psa_a = 0
        g_psa_d = 0
        g_ptps = 0
        g_ae = 0

        kh_v[0] = P.dma("sync", khb[0][:], kTd[0:128, :], s_w)
        vh_v[0] = P.dma("sync", vhb[0][:].rearrange("p (a e) -> p a e", e=128),
                        vd.rearrange("(a p) e -> p a e", p=128)[:, :, 0:128], s_w)

        for idx, (h, j) in enumerate(items):
            ib = idx % 2
            hb = h % 2
            kext = (2 * j + 2) * 128
            nk = kext // 128
            if j == 0 and h + 1 < H:
                nb = (h + 1) % 2
                kh_v[nb] = P.dma("sync", khb[nb][:],
                                 kTd[(h + 1) * 128:(h + 2) * 128, :], s_w,
                                 wait=[(s_pe, g_kh[nb])])
                vh_v[nb] = P.dma("sync", vhb[nb][:].rearrange("p (a e) -> p a e", e=128),
                                 vd.rearrange("(a p) e -> p a e", p=128)[:, :,
                                     (h + 1) * 128:(h + 2) * 128], s_w,
                                 wait=[(s_pe, g_vh[nb])])

            def fsc_mm(e, h=h, j=j, hb=hb, kext=kext):
                last = None
                for c in range((kext + 511) // 512):
                    w = min(512, kext - c * 512)
                    last = e.matmul(PSA[:, c * 512:c * 512 + w],
                                    qres[:, h * 1024 + j * 128:h * 1024 + (j + 1) * 128],
                                    khb[hb][:, c * 512:c * 512 + w],
                                    start=True, stop=True)
                return last
            vscm = P.op("tensor", fsc_mm, sem=s_pe,
                        wait=[(s_w, kh_v[hb]), (s_ac, g_psa_a), (s_dv, g_psa_d)])
            g_kh[hb] = vscm

            if kext > 256:
                def fse(e, ib=ib, kext=kext):
                    return e.activation(out=scb[ib][:, 0:kext - 256],
                                        in_=PSA[:, 0:kext - 256], func=AF.Copy)
                vse = P.op("scalar", fse, sem=s_ac, wait=[(s_pe, vscm)])
            else:
                vse = s_ac.n
            g_psa_a = vse

            def fsm(e, ib=ib, j=j, kext=kext):
                return e.scalar_tensor_tensor(out=scb[ib][:, kext - 256:kext],
                                              in0=masks[:, j * 256:(j + 1) * 256],
                                              scalar=NEG, in1=PSA[:, kext - 256:kext],
                                              op0=ALU.mult, op1=ALU.add)
            vsm = P.op("vector", fsm, sem=s_dv,
                       wait=[(s_pe, vscm), (s_ac, g_exp[ib])])
            g_psa_d = vsm

            def fmx(e, ib=ib, kext=kext):
                return e.tensor_reduce(out=nm_t[ib][:], in_=scb[ib][:, 0:kext],
                                       axis=mybir.AxisListType.X, op=ALU.max, negate=True)
            vmx = P.op("vector", fmx, sem=s_dv, wait=[(s_ac, vse), (s_ac, g_exp[ib])])

            def fex(e, ib=ib, kext=kext):
                return e.activation(out=pbb[ib][:, 0:kext], in_=scb[ib][:, 0:kext],
                                    func=AF.Exp, bias=nm_t[ib][:, 0:1],
                                    accum_out=sum_t[ib][:, 0:1])
            vex = P.op("scalar", fex, sem=s_ac, wait=[(s_dv, vmx), (s_pe, g_tr[ib])])
            g_exp[ib] = vex

            def frc(e, ib=ib):
                return e.reciprocal(out=rc_t[ib][:], in_=sum_t[ib][:])
            P.op("vector", frc, sem=s_dv, wait=[(s_ac, vex)])

            def fnr(e, ib=ib, kext=kext):
                return e.tensor_scalar(out=pbb[ib][:, 0:kext], in0=pbb[ib][:, 0:kext],
                                       scalar1=rc_t[ib][:, 0:1], scalar2=None, op0=ALU.mult)
            vnr = P.op("vector", fnr, sem=s_dv)

            vpt_e = 0
            for half in range((nk + 7) // 8):
                nkk = min(8, nk - half * 8)

                def ftr2(e, ib=ib, half=half, nkk=nkk):
                    last = None
                    for kb in range(nkk):
                        last = e.transpose(ptps[:, kb * 128:(kb + 1) * 128],
                                           pbb[ib][:, (half * 8 + kb) * 128:
                                                   (half * 8 + kb + 1) * 128], ident_b[:])
                    return last
                vtr2 = P.op("tensor", ftr2, sem=s_pe,
                            wait=[(s_dv, vnr), (s_ac, g_ptps)])

                def fpe(e, ib=ib, half=half, nkk=nkk):
                    return e.activation(out=ptb[ib][:, half * 1024:half * 1024 + nkk * 128],
                                        in_=ptps[:, 0:nkk * 128], func=AF.Copy)
                vpt_e = P.op("scalar", fpe, sem=s_ac,
                             wait=[(s_pe, vtr2), (s_pe, g_av[ib])])
                g_ptps = vpt_e
            g_tr[ib] = s_pe.n

            def fav(e, ib=ib, hb=hb, nk=nk):
                last = None
                for kb in range(nk):
                    last = e.matmul(avv,
                                    r3(vhb[hb][:], 128)[:, kb, :],
                                    ptb[ib][:, kb * 128:(kb + 1) * 128],
                                    start=(kb == 0), stop=(kb == nk - 1))
                return last
            vav = P.op("tensor", fav, sem=s_pe,
                       wait=[(s_ac, vpt_e), (s_w, vh_v[hb]), (s_ac, g_ae)])
            g_vh[hb] = vav
            g_av[ib] = vav

            def fae(e, h=h, j=j):
                return e.activation(out=att[:, h * 1024 + j * 128:h * 1024 + (j + 1) * 128],
                                    in_=avv, func=AF.Identity, bias=bv_t[:, h:h + 1],
                                    scale=1.0)
            g_ae = P.op("scalar", fae, sem=s_ac, wait=[(s_pe, vav)])

        P.barrier([s_dv, s_ac, s_pe, s_io, s_w, s_sc])

        wof = cv(Ap, 0, 32 * 128, F32)
        wob = [cv(Ap, 16 * 1024, 32 * 128, BF16), cv(Ap, 24 * 1024, 32 * 128, BF16)]
        oev = [cv(Ap, 32 * 1024, 512, F32), cv(Ap, 34 * 1024, 512, F32)]
        xrr = [cv(Ap, 36 * 1024, 512, F32), cv(Ap, 38 * 1024, 512, F32)]
        xo2 = [cv(Ap, 40 * 1024, 512, F32), cv(Ap, 42 * 1024, 512, F32)]
        io_o = [0, 0]
        dv_w = [0, 0]
        for fc in range(32):
            bi = fc % 2
            vwf = P.dma("sync", wof[:], woi[:, fc], s_w, wait=[(s_dv, dv_w[bi])])

            def fcw(e, bi=bi):
                return e.tensor_copy(out=wob[bi][:], in_=wof[:])
            vcw = P.op("vector", fcw, sem=s_dv, wait=[(s_w, vwf), (s_pe, s_pe.n)])
            dv_w[bi] = vcw

            for hf in range(2):
                def fom(e, bi=bi, hf=hf):
                    last = None
                    for ge in range(32):
                        last = e.matmul(a_[hf], wob[bi][:, ge * 128:(ge + 1) * 128],
                                        att[:, ge * 1024 + hf * 512:ge * 1024 + (hf + 1) * 512],
                                        start=(ge == 0), stop=(ge == 31))
                    return last
                vom = P.op("tensor", fom, sem=s_pe, wait=[(s_dv, vcw), (s_ac, s_ac.n)])

                def foe(e, fc=fc, hf=hf):
                    return e.activation(out=oev[hf][:], in_=a_[hf], func=AF.Identity,
                                        bias=bo_t[:, fc:fc + 1], scale=1.0)
                voe = P.op("scalar", foe, sem=s_ac, wait=[(s_pe, vom), (s_dv, s_dv.n)])

                xv = r3(xmid[fc * 128:(fc + 1) * 128, hf * 1024:(hf + 1) * 1024], 128)
                vx = P.dma("gpsimd", r3(xrr[hf][:], 128), xv[:, 0:8:2, :], s_io,
                           wait=[(s_dv, s_dv.n)])

                def fxa(e, hf=hf):
                    return e.tensor_tensor(out=xo2[hf][:], in0=oev[hf][:], in1=xrr[hf][:],
                                           op=ALU.add)
                vxa = P.op("vector", fxa, sem=s_dv,
                           wait=[(s_ac, voe), (s_io, vx), (s_io, io_o[hf])])
                vxa2 = fence_v()
                io_o[hf] = P.dma("gpsimd",
                                 xmid2[fc * 128:(fc + 1) * 128, hf * 512:(hf + 1) * 512],
                                 xo2[hf][:], s_io, wait=[(s_dv, vxa2)])
        P.barrier([s_dv, s_ac, s_pe, s_io, s_w, s_sc])

        if dump == "xmid2":
            vtmp2 = cv(Ap, 0, 1024, F32)
            for g in range(NG_D):
                v0 = P.dma("gpsimd", vtmp2[:], xmid2[g * 128:(g + 1) * 128, :], s_io)
                P.dma("gpsimd", dbg[g * 128:(g + 1) * 128, :], vtmp2[:], s_io,
                      wait=[(s_io, v0)])
            P.waits("gpsimd", [(s_io, s_io.n), (s_w, s_w.n)])
            _emit(nc, P, ctx)
            return nc

        for t in range(2):
            ln_phase(xmid2, t * 512, lnsl, xg_out=xg_t[0:NG_D, :])
            P.barrier([s_dv, s_ac, s_pe, s_io])
            gate_up(first_tile=False)
            P.barrier([s_dv, s_ac, s_pe, s_w, s_io])
            down(False, xmid2, t * 512, oT, t * 512)
            P.barrier([s_dv, s_ac, s_pe, s_w, s_io, s_sc, s_gp])

        P.waits("gpsimd", [(s_io, s_io.n), (s_w, s_w.n), (s_sc, s_sc.n)])
        _emit(nc, P, ctx)
    return nc


def _emit(nc, P, ctx):
    block = ctx.enter_context(nc.Block())
    for name in ENGINES:
        deco = getattr(block, name)

        def body(e, ops=P.ops[name]):
            for f in ops:
                f(e)
        deco(body)


def _stage_gu(qw):
    by = np.ascontiguousarray(qw).view(np.uint8).reshape(D, I // 2)[PI_D]
    return np.ascontiguousarray(by.reshape(NG_D, 128, NS2, 128).transpose(1, 2, 0, 3))


def _stage_dq(qw):
    by = np.ascontiguousarray(qw).view(np.uint8).reshape(I, D // 2)[PI_I]
    return np.ascontiguousarray(by.reshape(NG_I, 128, D // 2).transpose(1, 0, 2))


def _stage_wT(W, perm_rows, perm_cols):
    wT = np.asarray(W).T
    if perm_rows:
        wT = wT[PI_D]
    if perm_cols:
        wT = wT[:, PI_D]
    return np.ascontiguousarray(
        wT.reshape(32, 128, 32, 128).transpose(1, 2, 0, 3), dtype=np.float32)


def _stage_wv(W):
    wT = np.asarray(W).T[PI_D]
    return np.ascontiguousarray(wT.reshape(NG_D, 128, D).transpose(1, 0, 2),
                                dtype=np.float32)


def _stage_b(b, perm):
    bb = np.asarray(b)[PI_D] if perm else np.asarray(b)
    return np.ascontiguousarray(bb.reshape(32, 128).T, dtype=np.float32)


def _qz_bytes(qz):
    return np.ascontiguousarray(np.ascontiguousarray(qz).view(np.uint8).reshape(
        qz.shape[0], -1))


def stage_core(c, x, w):
    b, p = c // 2, c % 2
    blocks = np.arange(16)
    if p:
        blocks = blocks.reshape(8, 2)[:, ::-1].reshape(-1)
    cols = (blocks[:, None] * 128 + np.arange(128)[None, :]).reshape(-1)
    m = {
        "xT": np.ascontiguousarray(np.asarray(x[b]).T[PI_D][:, cols], dtype=np.float32),
        "qpos": np.ascontiguousarray(
            cols.reshape(16, 128)[0::2].T, dtype=np.float32),
        "kpos": np.ascontiguousarray(cols[None, :], dtype=np.float32),
    }
    m.update(w)
    return m, cols


def kernel(**inp):
    x = np.asarray(inp["L0_attn_x"], dtype=np.float32)
    w = {
        "qg": _stage_gu(inp["L0_ffn_gate_qweight"]),
        "qu": _stage_gu(inp["L0_ffn_up_qweight"]),
        "sg": np.ascontiguousarray(np.asarray(inp["L0_ffn_gate_scales"],
                                              dtype=np.float32)[:, PI_I]),
        "su": np.ascontiguousarray(np.asarray(inp["L0_ffn_up_scales"],
                                              dtype=np.float32)[:, PI_I]),
        "zgq": _qz_bytes(inp["L0_ffn_gate_qzeros"]),
        "zuq": _qz_bytes(inp["L0_ffn_up_qzeros"]),
        "qd": _stage_dq(inp["L0_ffn_down_qweight"]),
        "sd": np.ascontiguousarray(np.asarray(inp["L0_ffn_down_scales"],
                                              dtype=np.float32)[:, PI_D]),
        "zdq": _qz_bytes(inp["L0_ffn_down_qzeros"]),
        "wq": _stage_wT(inp["L0_attn_Wq"], True, False),
        "wk": _stage_wT(inp["L0_attn_Wk"], True, False),
        "wo": _stage_wT(inp["L0_attn_Wo"].T if False else inp["L0_attn_Wo"],
                        False, True),
        "wv": _stage_wv(inp["L0_attn_Wv"]),
        "bq": _stage_b(inp["L0_attn_bq"], False),
        "bk": _stage_b(inp["L0_attn_bk"], False),
        "bv": _stage_b(inp["L0_attn_bv"], False),
        "bo": _stage_b(inp["L0_attn_bo"], True),
    }
    nc = build_program()
    in_maps = []
    colmaps = []
    for c in range(8):
        m, cols = stage_core(c, x, w)
        in_maps.append(m)
        colmaps.append(cols)
    res = run_bass_kernel_spmd(nc, in_maps, core_ids=list(range(8)))

    out = np.empty((4, L, D), dtype=np.float32)
    inv = np.empty(D, dtype=np.int64)
    inv[PI_D] = np.arange(D)
    for c in range(8):
        b = c // 2
        own = colmaps[c].reshape(16, 128)[0::2].reshape(-1)
        oTc = np.asarray(res.results[c]["oT"])
        out[b, own, :] = oTc[inv].T
    return out


# version 03
# speedup vs baseline: 1.2821x; 1.2821x over previous
import os
import numpy as np
import ml_dtypes
from contextlib import ExitStack

import concourse.bass as bass
import concourse.mybir as mybir
from concourse.bass_utils import run_bass_kernel_spmd

F32 = mybir.dt.float32
BF16 = mybir.dt.bfloat16
U8 = mybir.dt.uint8
AF = mybir.ActivationFunctionType
ALU = mybir.AluOpType
BF = ml_dtypes.bfloat16

D = 4096
I = 11008
H = 32
L = 2048
OWN = 1024
NG_D = 32
NG_I = 86
NS2 = 43
NBLK = 8
EPS = 1e-5
QSCL = float(1.0 / np.sqrt(128.0))
NEG = -1.0e9

ENGINES = ("sync", "tensor", "vector", "scalar", "gpsimd")


def _pi(n):
    p = np.arange(n)
    blk, k = p // 128, p % 128
    return np.where(k < 64, blk * 128 + 2 * k, blk * 128 + 2 * (k - 64) + 1)


PI_D = _pi(D)
PI_I = _pi(I)


class Sem:
    def __init__(self, ctx, nc, name):
        self.h = ctx.enter_context(nc.semaphore(name))
        self.n = 0


SER_MASK = int(os.environ.get("KSER_MASK", "424"))
PH_INIT, PH_DQG, PH_DQD, PH_LN, PH_GU, PH_DOWN, PH_KQV, PH_ATT, PH_OPROJ = range(9)


class Prog:
    def __init__(self):
        self.ops = {e: [] for e in ENGINES}
        self.all_sems = []
        self.eng_sem = {}
        self.serial = True

    def phase(self, ph):
        self.serial = bool((SER_MASK >> ph) & 1)

    def _serial_wait(self):
        return [self.last] if getattr(self, "last", None) else []

    def emit(self, eng, fn):
        self.ops[eng].append(fn)

    def dma(self, q, out, in_, sem, wait=()):
        w = tuple(self._serial_wait()) + tuple(wait) if self.serial else tuple(wait)
        self.last = (sem, sem.n + 16)

        def f(e, out=out, in_=in_, w=w, sem=sem):
            for (s, v) in w:
                e.wait_ge(s.h, v)
            e.dma_start(out=out, in_=in_).then_inc(sem.h, 16)

        self.emit(q, f)
        sem.n += 16
        return sem.n

    def op(self, eng, fn, sem=None, inc=1, wait=()):
        if self.serial:
            wait = tuple(self._serial_wait()) + tuple(wait)
            if sem is None:
                sem = self.eng_sem[eng]
        if sem is not None:
            self.last = (sem, sem.n + inc)
        w = tuple(wait)

        def f(e, fn=fn, w=w, sem=sem, inc=inc):
            for (s, vv) in w:
                e.wait_ge(s.h, vv)
            ins = fn(e)
            if sem is not None:
                ins.then_inc(sem.h, inc)

        self.emit(eng, f)
        if sem is not None:
            sem.n += inc
            return sem.n
        return None

    def waits(self, eng, wait):
        w = tuple(wait)

        def f(e, w=w):
            for (s, v) in w:
                e.wait_ge(s.h, v)

        self.emit(eng, f)

    def barrier(self, sems):
        w = [(s, s.n) for s in sems]
        for e in ENGINES:
            self.waits(e, w)


def r3(ap, k):
    return ap.rearrange("p (a b) -> p a b", b=k)


def build_program(dump=None):
    nc = bass.Bass("TRN2", target_bir_lowering=False)

    din = lambda n, s, t: nc.dram_tensor(n, s, t, kind="ExternalInput").ap()
    xT = din("xT", [D, L], F32)
    qgi = din("qg", [128, NS2, NG_D, 128], U8)
    qui = din("qu", [128, NS2, NG_D, 128], U8)
    sgi = din("sg", [NG_D, I], F32)
    sui = din("su", [NG_D, I], F32)
    zgq = din("zgq", [NG_D, I // 2], U8)
    zuq = din("zuq", [NG_D, I // 2], U8)
    qdi = din("qd", [128, NG_I, D // 2], U8)
    sdi = din("sd", [NG_I, D], F32)
    zdq = din("zdq", [NG_I, D // 2], U8)
    wqi = din("wq", [128, 32, NG_D, 128], F32)
    wki = din("wk", [128, 32, NG_D, 128], F32)
    woi = din("wo", [128, 32, 32, 128], F32)
    wvi = din("wv", [128, NG_D, D], F32)
    bqi = din("bq", [128, 32], F32)
    bki = din("bk", [128, 32], F32)
    bvi = din("bv", [128, 32], F32)
    boi = din("bo", [128, 32], F32)
    qposi = din("qpos", [128, NBLK], F32)
    kposi = din("kpos", [1, L], F32)

    oT = nc.dram_tensor("oT", [D, OWN], F32, kind="ExternalOutput").ap()
    dbg = None
    if dump == "xmid":
        dbg = nc.dram_tensor("dbg", [D, L], F32, kind="ExternalOutput").ap()
    elif dump == "xmid2":
        dbg = nc.dram_tensor("dbg", [D, OWN], F32, kind="ExternalOutput").ap()
    elif dump == "kv":
        dbg = nc.dram_tensor("dbg", [2 * D, L], F32, kind="ExternalOutput").ap()
    elif dump == "ln0":
        dbg = nc.dram_tensor("dbg", [128, NG_D * 512], BF16, kind="ExternalOutput").ap()
        dbg2 = nc.dram_tensor("dbg2", [NG_D, 512], BF16, kind="ExternalOutput").ap()
        dbg3 = nc.dram_tensor("dbg3", [1, 2048], F32, kind="ExternalOutput").ap()
    elif dump == "h0":
        dbg = nc.dram_tensor("dbg", [128, NG_I * 512], BF16, kind="ExternalOutput").ap()
        dbg2 = nc.dram_tensor("dbg2", [NG_I, 512], BF16, kind="ExternalOutput").ap()
    elif dump == "od":
        dbg = nc.dram_tensor("dbg", [128, 8 * 4 * 512], BF16, kind="ExternalOutput").ap()
    elif dump == "dq":
        dbg = nc.dram_tensor("dbg", [128, 8, NG_D, 128], BF16, kind="ExternalOutput").ap()
        dbg2 = nc.dram_tensor("dbg2", [128, 2, D], BF16, kind="ExternalOutput").ap()
        dbg3 = nc.dram_tensor("dbg3", [NG_D, 256], BF16, kind="ExternalOutput").ap()

    cg = nc.dram_tensor("cg", [128, NG_I, NG_D, 128], BF16, kind="Internal").ap()
    cu = nc.dram_tensor("cu", [128, NG_I, NG_D, 128], BF16, kind="Internal").ap()
    cd = nc.dram_tensor("cd", [128, NG_I, D], BF16, kind="Internal").ap()
    zs_g = nc.dram_tensor("zs_g", [NG_D, I], BF16, kind="Internal").ap()
    zs_u = nc.dram_tensor("zs_u", [NG_D, I], BF16, kind="Internal").ap()
    zs_d = nc.dram_tensor("zs_d", [NG_I, D], BF16, kind="Internal").ap()
    xmid = nc.dram_tensor("xmid", [D, L], F32, kind="Internal").ap()
    xmid2 = nc.dram_tensor("xmid2", [D, OWN], F32, kind="Internal").ap()
    kTd = nc.dram_tensor("kTd", [D, L], BF16, kind="Internal").ap()
    vd = nc.dram_tensor("vd", [L, D], BF16, kind="Internal").ap()

    P = Prog()

    with ExitStack() as ctx:
        sb = lambda n, s, t: ctx.enter_context(nc.sbuf_tensor(n, s, t))

        s_w = Sem(ctx, nc, "s_w")
        s_io = Sem(ctx, nc, "s_io")
        s_sc = Sem(ctx, nc, "s_sc")
        s_pe = Sem(ctx, nc, "s_pe")
        s_dv = Sem(ctx, nc, "s_dv")
        s_ac = Sem(ctx, nc, "s_ac")
        s_gp = Sem(ctx, nc, "s_gp")
        P.all_sems = [s_w, s_io, s_sc, s_pe, s_dv, s_ac, s_gp]
        P.eng_sem = {"tensor": s_pe, "vector": s_dv, "scalar": s_ac,
                     "gpsimd": s_gp, "sync": s_w}

        Ap = sb("poolA", [128, 48 * 1024], U8)
        Bp = sb("poolB", [128, 64 * 1024], U8)
        Cp = sb("poolC", [128, 87 * 1024], U8)
        ones_f = sb("ones_f", [1, 128], F32)
        ones_b = sb("ones_b", [128, 1], BF16)
        iota_p = sb("iota_p", [128, 1], F32)
        iota256 = sb("iota256", [128, 256], F32)
        ident_b = sb("ident_b", [128, 128], BF16)
        bq_t = sb("bq_t", [128, 32], F32)
        bk_t = sb("bk_t", [128, 32], F32)
        bv_t = sb("bv_t", [128, 32], F32)
        bo_t = sb("bo_t", [128, 32], F32)
        qpos_t = sb("qpos_t", [128, NBLK], F32)
        masks = sb("masks", [128, NBLK * 256], BF16)
        cvals = sb("cvals", [1, 8], F32)
        CONSTS = (EPS, 1.0 / D, -0.5, QSCL, 1.0, 0.5)
        for _i, _v in enumerate(CONSTS):
            nc.const_aps.register(F32, _v, cvals[0:1, _i:_i + 1])
        gs32 = sb("gs32", [128, 2 * NG_D - 1], BF16)
        gs86 = sb("gs86", [128, 2 * NG_I - 1], BF16)

        fen_v = sb("fen_v", [1, 2], F32)
        fen_a = sb("fen_a", [1, 2], F32)

        def fence_v(wait=()):
            return P.op("vector", lambda e: e.memset(fen_v[0:1, :], 0.0), sem=s_dv,
                        wait=wait)

        def fence_a(wait=()):
            return P.op("scalar", lambda e: e.memzero(fen_a[0:1, :]), sem=s_ac,
                        wait=wait)

        PSA = ctx.enter_context(nc.psum_tensor("PSA", [128, 2048], F32))
        PSB = ctx.enter_context(nc.psum_tensor("PSB", [128, 2048], F32))
        a_ = [PSA[:, i * 512:(i + 1) * 512] for i in range(4)]
        b_ = [PSB[:, i * 512:(i + 1) * 512] for i in range(4)]

        def cv(pool, off, cols, dt):
            return pool[:, off:off + cols * mybir.dt.size(dt)].bitcast(dt)

        def g32sl(g):
            return gs32[:, NG_D - 1 - g:2 * NG_D - 1 - g]

        def g86sl(n):
            return gs86[:, NG_I - 1 - n:2 * NG_I - 1 - n]

        P.phase(PH_INIT)

        def f_cvals(e):
            last = None
            for _i, _v in enumerate(CONSTS):
                last = e.memset(cvals[0:1, _i:_i + 1], _v)
            return last
        P.op("vector", f_cvals, sem=s_dv)

        def f_iota(e):
            e.iota(iota_p[:], pattern=[[0, 1]], base=0, channel_multiplier=1,
                   allow_small_or_imprecise_dtypes=True)
            return e.iota(iota256[:], pattern=[[1, 256]], base=0, channel_multiplier=0,
                          allow_small_or_imprecise_dtypes=True)
        P.op("gpsimd", f_iota, sem=s_gp)

        kpos_t = cv(Bp, 56 * 1024, L, F32)[0:1, :]
        for t, srci in ((bq_t[:], bqi), (bk_t[:], bki), (bv_t[:], bvi), (bo_t[:], boi),
                        (qpos_t[:], qposi), (kpos_t, kposi)):
            P.dma("sync", t, srci, s_w)
        v_bias = s_w.n

        def f_const(e):
            e.memset(ones_f[:], 1.0)
            e.memset(ones_b[:], 1.0)
            e.memset(gs32[:], 0.0)
            e.memset(gs86[:], 0.0)
            e.memset(gs32[:, NG_D - 1:NG_D], -1.0)
            e.memset(gs86[:, NG_I - 1:NG_I], -1.0)
            return e.tensor_scalar(out=ident_b[:], in0=iota256[:, 0:128],
                                   scalar1=iota_p[:, 0:1], scalar2=None, op0=ALU.is_equal)
        P.op("vector", f_const, sem=s_dv, wait=[(s_gp, 1)])

        def f_krep(e):
            last = None
            for j in range(NBLK):
                last = e.matmul(PSA[:, j * 256:(j + 1) * 256], ones_f[:],
                                kpos_t[0:1, 2 * j * 128:2 * j * 128 + 256],
                                start=True, stop=True)
            return last
        vkr = P.op("tensor", f_krep, sem=s_pe, wait=[(s_w, v_bias), (s_dv, 1)])

        def f_masks(e):
            last = None
            for j in range(NBLK):
                last = e.tensor_scalar(
                    out=masks[:, j * 256:(j + 1) * 256],
                    in0=PSA[:, j * 256:(j + 1) * 256],
                    scalar1=qpos_t[:, j:j + 1], scalar2=None, op0=ALU.is_gt)
            return last
        P.op("vector", f_masks, sem=s_dv, wait=[(s_pe, vkr)])

        def f_bq(e):
            return e.tensor_scalar(out=bq_t[:], in0=bq_t[:], scalar1=QSCL, scalar2=None,
                                   op0=ALU.mult)
        P.op("vector", f_bq, sem=s_dv, wait=[(s_w, v_bias)])

        zrow = cv(Ap, 0, I // 2, U8)
        zlo = cv(Ap, 8 * 1024, I // 2, U8)
        zhi = cv(Ap, 16 * 1024, I // 2, U8)
        sful = cv(Bp, 0, I, F32)
        zout = cv(Cp, 0, I, BF16)

        def zbuild(qz, s_ap, zdram, npart, ncol):
            half = ncol // 2
            guard = [(s_dv, s_dv.n)]
            v0 = P.dma("gpsimd", zrow[0:npart, 0:half], qz, s_io, wait=guard)
            v1 = P.dma("sync", sful[0:npart, 0:ncol], s_ap, s_w, wait=guard)

            def f(e):
                e.tensor_scalar(out=zlo[0:npart, 0:half], in0=zrow[0:npart, 0:half],
                                scalar1=15, scalar2=None, op0=ALU.bitwise_and)
                e.tensor_scalar(out=zhi[0:npart, 0:half], in0=zrow[0:npart, 0:half],
                                scalar1=4, scalar2=None, op0=ALU.logical_shift_right)
                z3 = r3(zout[0:npart, 0:ncol], 128)
                s3 = r3(sful[0:npart, 0:ncol], 128)
                l3 = r3(zlo[0:npart, 0:half], 64)
                h3 = r3(zhi[0:npart, 0:half], 64)
                e.tensor_tensor(out=z3[:, :, 0:64], in0=l3, in1=s3[:, :, 0:64], op=ALU.mult)
                return e.tensor_tensor(out=z3[:, :, 64:128], in0=h3, in1=s3[:, :, 64:128],
                                       op=ALU.mult)
            P.op("vector", f, sem=s_dv, wait=[(s_io, v0), (s_w, v1)])
            vv = fence_v()
            return P.dma("gpsimd", zdram, zout[0:npart, 0:ncol], s_io, wait=[(s_dv, vv)])

        zbuild(zgq, sgi, zs_g, NG_D, I)
        zbuild(zuq, sui, zs_u, NG_D, I)
        zbuild(zdq, sdi, zs_d, NG_I, D)
        P.barrier([s_io, s_dv, s_w, s_gp, s_pe])

        cw_gate = [0] * NG_I
        cw_up = [0] * NG_I
        cw_down = [0] * NG_I
        qb2 = [cv(Ap, 0, 4096, U8), cv(Ap, 4 * 1024, 4096, U8)]
        ulo = cv(Ap, 8 * 1024, 4096, U8)
        uhi = cv(Ap, 12 * 1024, 4096, U8)
        dqs = [cv(Bp, 0, 32 * 256, BF16), cv(Bp, 16 * 1024, 32 * 256, BF16),
               cv(Bp, 32 * 1024, 32 * 256, BF16), cv(Bp, 48 * 1024, 32 * 256, BF16)]
        srw = [cv(Ap, 16 * 1024, 256, F32)[0:1, :], cv(Ap, 17 * 1024, 256, F32)[0:1, :]]

        def dq_matrix(qsrc, ssrc, cdst, cwlist, s0, s1, qbufs, slabbufs):
            dv_after = [0] * NS2
            for s in range(NS2):
                bi = s % 2
                vq = P.dma("gpsimd", qbufs[bi][:], qsrc[:, s], s_io,
                           wait=[(s_dv, dv_after[s - 2])] if s >= 2 else [])

                def fum(e, qb=qbufs[bi]):
                    e.tensor_scalar(out=ulo[:], in0=qb[:], scalar1=15, scalar2=None,
                                    op0=ALU.bitwise_and)
                    return e.tensor_scalar(out=uhi[:], in0=qb[:], scalar1=4,
                                           scalar2=None, op0=ALU.logical_shift_right)
                vum = P.op("vector", fum, sem=s_dv, wait=[(s_io, vq)])

                for g in range(NG_D):
                    sbi = g % 2
                    vsr = P.dma("sync", srw[sbi][0:1, :],
                                ssrc[g:g + 1, s * 256:(s + 1) * 256], s_sc,
                                wait=[(s_pe, s_pe.n)])

                    def frp(e, sbi=sbi, s0=s0, s1=s1):
                        return e.matmul((s0 if sbi == 0 else s1)[:, 0:256], ones_f[:],
                                        srw[sbi][0:1, :], start=True, stop=True)
                    vrp = P.op("tensor", frp, sem=s_pe, wait=[(s_sc, vsr), (s_dv, s_dv.n)])

                    def fsc(e, g=g, sbi=sbi, s0=s0, s1=s1, slab=slabbufs[bi]):
                        o4 = r3(slab[:], 32 * 128).rearrange("p a (g k) -> p a g k", k=128)
                        l3 = r3(ulo[:, g * 128:(g + 1) * 128], 64)
                        h3 = r3(uhi[:, g * 128:(g + 1) * 128], 64)
                        sr3 = r3((s0 if sbi == 0 else s1)[:, 0:256], 128)
                        e.tensor_tensor(out=o4[:, :, g, 0:64], in0=l3, in1=sr3[:, :, 0:64],
                                        op=ALU.mult)
                        return e.tensor_tensor(out=o4[:, :, g, 64:128], in0=h3,
                                               in1=sr3[:, :, 64:128], op=ALU.mult)
                    wfsc = [(s_pe, vrp), (s_dv, vum)]
                    if g == 0 and s >= 2:
                        wfsc.append((s_io, cwlist[2 * (s - 2)]))
                    P.op("vector", fsc, sem=s_dv, wait=wfsc)
                dv_after[s] = fence_v()
                vcw = P.dma("gpsimd", cdst[:, 2 * s:2 * s + 2],
                            r3(slabbufs[bi][:], 32 * 128), s_io, wait=[(s_dv, dv_after[s])])
                cwlist[2 * s] = vcw
                cwlist[2 * s + 1] = vcw

        P.phase(PH_DQG)
        dq_matrix(qgi, sgi, cg, cw_gate, b_[0], b_[1], qb2, dqs[0:2])
        dq_matrix(qui, sui, cu, cw_up, b_[0], b_[1], qb2[::-1], dqs[2:4])

        qrow = [cv(Ap, 0, 2048, U8), cv(Ap, 2 * 1024, 2048, U8)]
        dlo = cv(Ap, 4 * 1024, 2048, U8)
        dhi = cv(Ap, 6 * 1024, 2048, U8)
        dout = [cv(Ap, 8 * 1024, D, BF16), cv(Ap, 16 * 1024, D, BF16)]
        srd = [cv(Bp, 0, D, F32)[0:1, :], cv(Bp, 16 * 1024, D, F32)[0:1, :]]
        dv_aft = [0] * NG_I
        P.phase(PH_DQD)
        v_gu_done = [(s_io, s_io.n), (s_dv, s_dv.n)]
        for g in range(NG_I):
            bi = g % 2
            vs = P.dma("sync", srd[bi][0:1, :], sdi[g:g + 1, :], s_sc,
                       wait=[(s_pe, s_pe.n)] + ([] if g >= 2 else v_gu_done))
            vq = P.dma("gpsimd", qrow[bi][:], qdi[:, g, :], s_io,
                       wait=[(s_dv, dv_aft[g - 2])] if g >= 2 else v_gu_done)

            def fum(e, bi=bi):
                e.tensor_scalar(out=dlo[:], in0=qrow[bi][:], scalar1=15, scalar2=None,
                                op0=ALU.bitwise_and)
                return e.tensor_scalar(out=dhi[:], in0=qrow[bi][:], scalar1=4,
                                       scalar2=None, op0=ALU.logical_shift_right)
            vum = P.op("vector", fum, sem=s_dv, wait=[(s_io, vq)])

            for c in range(4):
                def frp(e, bi=bi, c=c):
                    e.matmul(b_[0][:], ones_f[:], srd[bi][0:1, c * 1024:c * 1024 + 512],
                             start=True, stop=True)
                    return e.matmul(b_[1][:], ones_f[:],
                                    srd[bi][0:1, c * 1024 + 512:(c + 1) * 1024],
                                    start=True, stop=True)
                vrp = P.op("tensor", frp, sem=s_pe, wait=[(s_sc, vs), (s_dv, s_dv.n)])

                def ftt(e, bi=bi, c=c):
                    o3 = r3(dout[bi][:, c * 1024:(c + 1) * 1024], 128)
                    l3 = r3(dlo[:, c * 512:(c + 1) * 512], 64)
                    h3 = r3(dhi[:, c * 512:(c + 1) * 512], 64)
                    s0 = r3(b_[0], 128)
                    s1 = r3(b_[1], 128)
                    e.tensor_tensor(out=o3[:, 0:4, 0:64], in0=l3[:, 0:4], in1=s0[:, :, 0:64],
                                    op=ALU.mult)
                    e.tensor_tensor(out=o3[:, 0:4, 64:128], in0=h3[:, 0:4],
                                    in1=s0[:, :, 64:128], op=ALU.mult)
                    e.tensor_tensor(out=o3[:, 4:8, 0:64], in0=l3[:, 4:8], in1=s1[:, :, 0:64],
                                    op=ALU.mult)
                    return e.tensor_tensor(out=o3[:, 4:8, 64:128], in0=h3[:, 4:8],
                                           in1=s1[:, :, 64:128], op=ALU.mult)
                wftt = [(s_pe, vrp), (s_dv, vum)]
                if c == 0 and g >= 2:
                    wftt.append((s_io, cw_down[g - 2]))
                P.op("vector", ftt, sem=s_dv, wait=wftt)
            dv_aft[g] = fence_v()
            cw_down[g] = P.dma("gpsimd", cd[:, g, :], dout[bi][:], s_io,
                               wait=[(s_dv, dv_aft[g])])

        if dump == "dq":
            dqtmp = cv(Bp, 0, 8 * NG_D * 128, BF16)
            v0 = P.dma("gpsimd", dqtmp[:].rearrange("p (a g k) -> p a g k", g=NG_D, k=128),
                       cg[:, 78:86], s_io, wait=[(s_io, s_io.n)])
            P.dma("gpsimd", dbg, dqtmp[:].rearrange("p (a g k) -> p a g k", g=NG_D, k=128),
                  s_io, wait=[(s_io, v0)])
            dqt2 = cv(Bp, 32 * 1024, 2 * D, BF16)
            v0 = P.dma("gpsimd", dqt2[:].rearrange("p (a k) -> p a k", k=D), cd[:, 0:2], s_io)
            P.dma("gpsimd", dbg2, dqt2[:].rearrange("p (a k) -> p a k", k=D), s_io,
                  wait=[(s_io, v0)])
            dqt3 = cv(Bp, 40 * 1024, 256, BF16)
            v0 = P.dma("gpsimd", dqt3[0:NG_D, :], zs_g[:, 0:256], s_io)
            P.dma("gpsimd", dbg3, dqt3[0:NG_D, :], s_io, wait=[(s_io, v0)])
            P.waits("gpsimd", [(s_io, s_io.n), (s_w, s_w.n)])
            _emit(nc, P, ctx)
            return nc

        if int(os.environ.get("KDQBAR", "0")):
            P.barrier([s_dv, s_ac, s_pe, s_w, s_io, s_sc, s_gp])

        st_row = cv(Ap, 0, 2048, F32)[0:1, :]
        xck = [cv(Ap, 24 * 1024, 512, F32), cv(Ap, 26 * 1024, 512, F32)]
        x2s = [cv(Ap, 28 * 1024, 512, BF16), cv(Ap, 29 * 1024, 512, BF16)]

        def ln_phase(src, c0, ln_slot, xg_out=None):
            P.phase(PH_LN)
            ps_s1 = b_[0][0:1, :]
            ps_s2 = b_[1][0:1, :]
            ps_rep = b_[2]
            ps_rep2 = b_[3]
            ps_xgv = b_[0][0:NG_D, :]
            dvh = [s_dv.n, s_dv.n]
            for g in range(NG_D):
                bi = g % 2
                vx = P.dma("gpsimd", xck[bi][:], src[g * 128:(g + 1) * 128, c0:c0 + 512],
                           s_io, wait=[(s_dv, dvh[bi])])

                def fcv(e, g=g, bi=bi):
                    return e.tensor_copy(out=ln_slot(g), in_=xck[bi][:])
                dvh[bi] = P.op("vector", fcv, sem=s_dv, wait=[(s_io, vx)])

                def fsq(e, g=g, bi=bi):
                    return e.activation(out=x2s[bi][:], in_=ln_slot(g), func=AF.Square)
                vsq = P.op("scalar", fsq, sem=s_ac,
                           wait=[(s_dv, dvh[bi]), (s_pe, s_pe.n)])

                def fst(e, g=g, bi=bi):
                    e.matmul(ps_s1, ones_b[:], ln_slot(g),
                             start=(g == 0), stop=(g == NG_D - 1))
                    return e.matmul(ps_s2, ones_b[:], x2s[bi][:],
                                    start=(g == 0), stop=(g == NG_D - 1))
                P.op("tensor", fst, sem=s_pe, wait=[(s_ac, vsq)])
            v_stats = s_pe.n

            st_m = st_row[0:1, 0:512]
            st_v = st_row[0:1, 512:1024]
            st_r = st_row[0:1, 1024:1536]
            st_mr = st_row[0:1, 1536:2048]

            def fev(e):
                e.activation(out=st_m, in_=ps_s1, func=AF.Copy, scale=1.0 / D)
                return e.activation(out=st_v, in_=ps_s2, func=AF.Copy, scale=1.0 / D)
            vev = P.op("scalar", fev, sem=s_ac, wait=[(s_pe, v_stats)])

            def fvar(e):
                e.tensor_tensor(out=st_mr, in0=st_m, in1=st_m, op=ALU.mult)
                return e.tensor_tensor(out=st_v, in0=st_v, in1=st_mr, op=ALU.subtract)
            vva = P.op("vector", fvar, sem=s_dv, wait=[(s_ac, vev)])

            def frs(e):
                e.activation(out=st_r, in_=st_v, func=AF.Ln, bias=EPS)
                return e.activation(out=st_r, in_=st_r, func=AF.Exp, scale=-0.5)
            vrs = P.op("scalar", frs, sem=s_ac, wait=[(s_dv, vva)])

            def fmr(e):
                return e.tensor_tensor(out=st_mr, in0=st_m, in1=st_r, op=ALU.mult)
            vmr = P.op("vector", fmr, sem=s_dv, wait=[(s_ac, vrs)])

            def frep(e):
                e.matmul(ps_rep, ones_f[:], st_r, start=True, stop=True)
                return e.matmul(ps_rep2, ones_f[:], st_mr, start=True, stop=True)
            vrep = P.op("tensor", frep, sem=s_pe, wait=[(s_dv, vmr)])

            for g in range(NG_D):
                def fnm(e, g=g):
                    sl = ln_slot(g)
                    e.tensor_tensor(out=sl, in0=sl, in1=ps_rep, op=ALU.mult)
                    return e.tensor_tensor(out=sl, in0=sl, in1=ps_rep2, op=ALU.subtract)
                vn = P.op("vector", fnm, sem=s_dv, wait=[(s_pe, vrep)])
                if xg_out is not None:
                    def fxg(e, g=g):
                        return e.matmul(ps_xgv, g32sl(g), ln_slot(g),
                                        start=(g == 0), stop=(g == NG_D - 1))
                    P.op("tensor", fxg, sem=s_pe, wait=[(s_dv, vn)])
            if xg_out is not None:
                def fxe(e):
                    return e.activation(out=xg_out, in_=ps_xgv, func=AF.Copy)
                P.op("scalar", fxe, sem=s_ac, wait=[(s_pe, s_pe.n)])

        ln_t = cv(Bp, 0, NG_D * 512, BF16)
        lnsl = lambda g: ln_t[:, g * 512:(g + 1) * 512]
        gslab = [cv(Bp, 32 * 1024, 32 * 128, BF16), cv(Bp, 40 * 1024, 32 * 128, BF16)]
        uslab = [cv(Bp, 48 * 1024, 32 * 128, BF16), cv(Bp, 56 * 1024, 32 * 128, BF16)]
        hbuf = cv(Cp, 0, NG_I * 512, BF16)
        xg_t = cv(Ap, 30 * 1024, 512, BF16)
        zscg = [cv(Ap, 31 * 1024, 128, BF16), cv(Ap, 31 * 1024 + 256, 128, BF16)]
        zscu = [cv(Ap, 31 * 1024 + 512, 128, BF16), cv(Ap, 31 * 1024 + 768, 128, BF16)]
        hg_o = cv(Ap, 32 * 1024, 512, BF16)
        sil = [cv(Ap, 33 * 1024, 512, BF16), cv(Ap, 34 * 1024, 512, BF16)]
        wdb = [cv(Ap, 35 * 1024, 512, BF16), cv(Ap, 36 * 1024, 512, BF16)]
        zsdc = cv(Ap, 37 * 1024, 512, BF16)
        obf = cv(Ap, 38 * 1024, 4 * 512, BF16)
        xres = [cv(Ap, 42 * 1024, 512, F32), cv(Ap, 44 * 1024, 512, F32)]
        xoutb = [cv(Ap, 16 * 1024, 512, F32), cv(Ap, 18 * 1024, 512, F32)]

        def gate_up(first_tile):
            P.phase(PH_GU)
            ps_gv = [a_[0], a_[1]]
            ps_uv = [a_[2], a_[3]]
            ps_hgv = b_[0][0:NG_I, :]
            ev_g = [0, 0]
            ev_u = [0, 0]
            pe_aft = [0] * NG_I
            for n in range(NG_I):
                bi = n % 2
                wsl = [(s_pe, pe_aft[n - 2])] if n >= 2 else []
                if first_tile:
                    wsl = wsl + [(s_io, cw_gate[n])]
                P.dma("sync", gslab[bi][:], cg[:, n], s_w, wait=wsl)
                wsl2 = [(s_io, cw_up[n])] if first_tile else []
                P.dma("sync", uslab[bi][:], cu[:, n], s_w, wait=wsl2)
                P.dma("sync", zscg[bi][0:NG_D, :], zs_g[:, n * 128:(n + 1) * 128], s_w)
                vz = P.dma("sync", zscu[bi][0:NG_D, :], zs_u[:, n * 128:(n + 1) * 128], s_w)

                def fmm(e, n=n, bi=bi):
                    for g in range(NG_D):
                        e.matmul(ps_gv[bi], gslab[bi][:, g * 128:(g + 1) * 128],
                                 lnsl(g), start=(g == 0), stop=False)
                        e.matmul(ps_uv[bi], uslab[bi][:, g * 128:(g + 1) * 128],
                                 lnsl(g), start=(g == 0), stop=False)
                    e.matmul(ps_gv[bi], zscg[bi][0:NG_D, :], xg_t[0:NG_D, :],
                             start=False, stop=True)
                    return e.matmul(ps_uv[bi], zscu[bi][0:NG_D, :], xg_t[0:NG_D, :],
                                    start=False, stop=True)
                vmm = P.op("tensor", fmm, sem=s_pe,
                           wait=[(s_w, vz), (s_ac, ev_g[bi]), (s_dv, ev_u[bi])])
                pe_aft[n] = vmm

                def fsi(e, bi=bi):
                    return e.activation(out=sil[bi][:], in_=ps_gv[bi], func=AF.Silu)
                ev_g[bi] = P.op("scalar", fsi, sem=s_ac,
                                wait=[(s_pe, vmm), (s_dv, ev_u[bi])])

                def fh(e, n=n, bi=bi):
                    return e.tensor_tensor(out=hbuf[:, n * 512:(n + 1) * 512],
                                           in0=sil[bi][:], in1=ps_uv[bi], op=ALU.mult)
                ev_u[bi] = P.op("vector", fh, sem=s_dv, wait=[(s_ac, ev_g[bi])])

                def fhg(e, n=n):
                    return e.matmul(ps_hgv, g86sl(n),
                                    hbuf[:, n * 512:(n + 1) * 512],
                                    start=(n == 0), stop=(n == NG_I - 1))
                P.op("tensor", fhg, sem=s_pe, wait=[(s_dv, ev_u[bi])])

            def fhe(e):
                return e.activation(out=hg_o[0:NG_I, :], in_=b_[0][0:NG_I, :], func=AF.Copy)
            P.op("scalar", fhe, sem=s_ac, wait=[(s_pe, s_pe.n)])

        psb_bf = PSB[:].bitcast(BF16)

        DOWN_BARRIER = bool(int(os.environ.get("KDOWN_BARRIER", "0")))
        DOWN_TAIL = bool(int(os.environ.get("KDOWN_TAIL", "0")))

        def down(first_tile, x_src, c0s, x_dst, c0d, extra_dst=None, extra_c0=0):
            P.phase(PH_DOWN)
            ps_ov = [a_[0], a_[1], a_[2], a_[3]]
            ev = 0
            vst_l = 0
            io_x = [0, 0]
            pe_aft = [0] * NG_I
            for fs in range(8):
                if DOWN_BARRIER:
                    P.barrier([s_dv, s_ac, s_pe, s_w, s_io, s_sc])
                if DOWN_TAIL:
                    P.serial = False
                for g in range(NG_I):
                    bi = g % 2
                    wv_ = [(s_pe, pe_aft[g - 2])]
                    if first_tile and fs == 0:
                        wv_ = wv_ + [(s_io, cw_down[g])]
                    vw = P.dma("sync", wdb[bi][:], cd[:, g, fs * 512:(fs + 1) * 512],
                               s_w, wait=wv_)

                    def fdm(e, g=g, bi=bi):
                        last = None
                        for tc in range(4):
                            last = e.matmul(ps_ov[tc],
                                            hbuf[:, g * 512 + tc * 128:g * 512 + tc * 128 + 128],
                                            wdb[bi][:], start=(g == 0), stop=False)
                        return last
                    pe_aft[g] = P.op("tensor", fdm, sem=s_pe,
                                     wait=[(s_w, vw)] + ([(s_ac, ev)] if g == 0 else []))

                if DOWN_TAIL:
                    P.serial = True
                vzd = P.dma("sync", zsdc[0:NG_I, :], zs_d[:, fs * 512:(fs + 1) * 512], s_w,
                            wait=[(s_pe, pe_aft[NG_I - 2])])

                def fcr(e):
                    last = None
                    for tc in range(4):
                        last = e.matmul(ps_ov[tc], hg_o[0:NG_I, tc * 128:(tc + 1) * 128],
                                        zsdc[0:NG_I, :], start=False, stop=True)
                    return last
                vcr = P.op("tensor", fcr, sem=s_pe, wait=[(s_w, vzd)])

                v_tr_prev = s_pe.n
                for tc in range(4):
                    def fev(e, tc=tc):
                        return e.activation(out=obf[:, tc * 512:(tc + 1) * 512],
                                            in_=ps_ov[tc], func=AF.Copy)
                    ev = P.op("scalar", fev, sem=s_ac, wait=[(s_pe, vcr)])
                v_ev = ev
                ev = fence_a()
                if dump == "od" and first_tile and c0d == 0:
                    vfa = fence_a()
                    P.dma("gpsimd", dbg[:, fs * 2048:(fs + 1) * 2048], obf[:], s_io,
                          wait=[(s_ac, vfa)])

                for fsub in range(4):
                    fglob = fs * 4 + fsub
                    bi = fglob % 2

                    def ftr(e, fsub=fsub):
                        last = None
                        for tc in range(4):
                            last = e.transpose(
                                psb_bf[:, tc * 128:(tc + 1) * 128],
                                obf[:, tc * 512 + fsub * 128:tc * 512 + fsub * 128 + 128],
                                ident_b[:])
                        return last
                    vtr = P.op("tensor", ftr, sem=s_pe,
                               wait=[(s_ac, v_ev), (s_dv, vst_l)])

                    vx = P.dma("gpsimd", xres[bi][:],
                               x_src[fglob * 128:(fglob + 1) * 128, c0s:c0s + 512], s_io,
                               wait=[(s_dv, s_dv.n)] if fglob >= 2 else [])

                    def fst(e, bi=bi):
                        return e.scalar_tensor_tensor(
                            out=xoutb[bi][:], in0=psb_bf[:, 0:512],
                            scalar=0.5, in1=xres[bi][:], op0=ALU.mult, op1=ALU.add)
                    P.op("vector", fst, sem=s_dv,
                         wait=[(s_pe, vtr), (s_io, vx), (s_io, io_x[bi])])
                    vst_l = fence_v()
                    io_x[bi] = P.dma("gpsimd",
                                     x_dst[fglob * 128:(fglob + 1) * 128, c0d:c0d + 512],
                                     xoutb[bi][:], s_io, wait=[(s_dv, vst_l)])
                    if extra_dst is not None:
                        io_x[bi] = P.dma(
                            "gpsimd",
                            extra_dst[fglob * 128:(fglob + 1) * 128, extra_c0:extra_c0 + 512],
                            xoutb[bi][:], s_io, wait=[(s_dv, vst_l)])

        for t in range(4):
            ln_phase(xT, t * 512, lnsl, xg_out=xg_t[0:NG_D, :])
            P.barrier([s_dv, s_ac, s_pe, s_io])
            if t == 0 and dump == "ln0":
                P.dma("gpsimd", dbg, ln_t[:], s_io)
                P.dma("gpsimd", dbg2, xg_t[0:NG_D, :], s_io)
                P.dma("gpsimd", dbg3, st_row[0:1, :], s_io)
                P.waits("gpsimd", [(s_io, s_io.n), (s_w, s_w.n)])
                _emit(nc, P, ctx)
                return nc
            gate_up(first_tile=(t == 0))
            P.barrier([s_dv, s_ac, s_pe, s_w, s_io])
            if t == 0 and dump == "h0":
                P.dma("gpsimd", dbg, hbuf[:], s_io)
                P.dma("gpsimd", dbg2, hg_o[0:NG_I, :], s_io)
                P.waits("gpsimd", [(s_io, s_io.n), (s_w, s_w.n)])
                _emit(nc, P, ctx)
                return nc
            down(t == 0, xT, t * 512, xmid, t * 512,
                 extra_dst=dbg if dump == "xmid" else None, extra_c0=t * 512)
            P.barrier([s_dv, s_ac, s_pe, s_w, s_io, s_sc, s_gp])
            if dump == "od" and t == 0:
                P.waits("gpsimd", [(s_io, s_io.n), (s_w, s_w.n)])
                _emit(nc, P, ctx)
                return nc

        if dump == "xmid":
            P.waits("gpsimd", [(s_io, s_io.n), (s_w, s_w.n)])
            _emit(nc, P, ctx)
            return nc

        ln2C = cv(Cp, 0, NG_D * 1024, BF16)
        qres = cv(Bp, 0, 32 * 1024, BF16)
        wslf = cv(Ap, 0, 32 * 128, F32)
        wbfs = [cv(Ap, 16 * 1024, 32 * 128, BF16), cv(Ap, 24 * 1024, 32 * 128, BF16)]
        kob = [cv(Ap, 32 * 1024, 512, BF16), cv(Ap, 33 * 1024, 512, BF16)]
        vvf = [cv(Ap, 34 * 1024, 512, F32), cv(Ap, 36 * 1024, 512, F32)]
        vvb2 = [cv(Ap, 38 * 1024, 512, BF16), cv(Ap, 39 * 1024, 512, BF16)]
        vob = [cv(Ap, 40 * 1024, 512, BF16), cv(Ap, 41 * 1024, 512, BF16)]

        for hf in range(2):
            ln_phase(xmid, hf * 1024, lambda g, hf=hf: ln2C[:, g * 1024:g * 1024 + 512])
            ln_phase(xmid, hf * 1024 + 512,
                     lambda g, hf=hf: ln2C[:, g * 1024 + 512:(g + 1) * 1024])
            P.barrier([s_dv, s_ac, s_pe, s_io, s_w, s_sc])

            P.phase(PH_KQV)
            io_k = [0, 0]
            for mat, bias_t, isq in ((wki, bk_t, False), (wqi, bq_t, True)):
                for ec in range(32):
                    bi = ec % 2
                    vwf = P.dma("sync", wslf[:], mat[:, ec], s_w, wait=[(s_dv, s_dv.n)])

                    def fcv2(e, bi=bi):
                        return e.tensor_copy(out=wbfs[bi][:], in_=wslf[:])
                    vcv = P.op("vector", fcv2, sem=s_dv,
                               wait=[(s_w, vwf), (s_pe, s_pe.n)])

                    if not isq:
                        for sub in range(2):
                            def fkm(e, bi=bi, sub=sub):
                                last = None
                                for g in range(NG_D):
                                    last = e.matmul(a_[sub],
                                                    wbfs[bi][:, g * 128:(g + 1) * 128],
                                                    ln2C[:, g * 1024 + sub * 512:
                                                         g * 1024 + (sub + 1) * 512],
                                                    start=(g == 0), stop=(g == NG_D - 1))
                                return last
                            vkm = P.op("tensor", fkm, sem=s_pe,
                                       wait=[(s_dv, vcv), (s_ac, s_ac.n)])

                            def fke(e, ec=ec, sub=sub, bias_t=bias_t):
                                return e.activation(out=kob[sub][:], in_=a_[sub],
                                                    func=AF.Identity,
                                                    bias=bias_t[:, ec:ec + 1], scale=1.0)
                            P.op("scalar", fke, sem=s_ac,
                                 wait=[(s_pe, vkm), (s_io, io_k[sub])])
                            vke = fence_a()
                            io_k[sub] = P.dma(
                                "gpsimd",
                                kTd[ec * 128:(ec + 1) * 128,
                                    hf * 1024 + sub * 512:hf * 1024 + (sub + 1) * 512],
                                kob[sub][:], s_ac if False else s_io,
                                wait=[(s_ac, vke)])
                    else:
                        def fqm(e, bi=bi):
                            last = None
                            for g in range(NG_D):
                                rhs = r3(ln2C[:, g * 1024:(g + 1) * 1024], 128)[:, 0:8:2, :]
                                last = e.matmul(a_[2], wbfs[bi][:, g * 128:(g + 1) * 128],
                                                rhs, start=(g == 0), stop=(g == NG_D - 1))
                            return last
                        vqm = P.op("tensor", fqm, sem=s_pe,
                                   wait=[(s_dv, vcv), (s_ac, s_ac.n)])

                        def fqe(e, ec=ec, hf=hf):
                            return e.activation(out=qres[:, ec * 1024 + hf * 512:
                                                        ec * 1024 + (hf + 1) * 512],
                                                in_=a_[2], func=AF.Identity,
                                                bias=bq_t[:, ec:ec + 1], scale=QSCL)
                        P.op("scalar", fqe, sem=s_ac, wait=[(s_pe, vqm)])

            io_vob = [0, 0]
            for eq in range(8):
                vg_after = [0, 0]
                for g in range(NG_D):
                    bi = g % 2
                    vwf = P.dma("sync", vvf[bi][:], wvi[:, g, eq * 512:(eq + 1) * 512],
                                s_w, wait=[(s_pe, vg_after[bi])])

                    def fcv3(e, bi=bi):
                        return e.tensor_copy(out=vvb2[bi][:], in_=vvf[bi][:])
                    vcv = P.op("vector", fcv3, sem=s_dv,
                               wait=[(s_w, vwf), (s_pe, vg_after[bi])])

                    def fvm(e, g=g, bi=bi, hf=hf):
                        last = None
                        for tb in range(8):
                            last = e.matmul(
                                (a_ + b_)[tb],
                                ln2C[:, g * 1024 + tb * 128:g * 1024 + tb * 128 + 128],
                                vvb2[bi][:], start=(g == 0), stop=(g == NG_D - 1))
                        return last
                    vg_after[bi] = P.op("tensor", fvm, sem=s_pe,
                                        wait=[(s_dv, vcv)] +
                                        ([(s_ac, s_ac.n)] if g == 0 else []))
                vvm = s_pe.n
                for tb in range(8):
                    bi2 = tb % 2

                    def fve(e, tb=tb, bi2=bi2):
                        return e.activation(out=vob[bi2][:], in_=(a_ + b_)[tb], func=AF.Copy)
                    P.op("scalar", fve, sem=s_ac,
                         wait=[(s_pe, vvm), (s_io, io_vob[bi2])])
                    vve = fence_a()
                    io_vob[bi2] = P.dma(
                        "gpsimd",
                        vd[hf * 1024 + tb * 128:hf * 1024 + (tb + 1) * 128,
                           eq * 512:(eq + 1) * 512],
                        vob[bi2][:], s_io, wait=[(s_ac, vve)])
            P.barrier([s_dv, s_ac, s_pe, s_io, s_w, s_sc])

        if dump == "kv":
            vtmp = cv(Ap, 0, 2048, BF16)
            vf32 = cv(Ap, 4 * 1024, 2048, F32)
            for g in range(NG_D):
                v0 = P.dma("gpsimd", vtmp[:], kTd[g * 128:(g + 1) * 128, :], s_io)

                def fc(e):
                    return e.tensor_copy(out=vf32[:], in_=vtmp[:])
                vv = P.op("vector", fc, sem=s_dv, wait=[(s_io, v0)])
                P.dma("gpsimd", dbg[g * 128:(g + 1) * 128, :], vf32[:], s_io,
                      wait=[(s_dv, vv)])
                v0 = P.dma("gpsimd", vtmp[:],
                           vd.rearrange("(a p) e -> p (a e)", p=128)[:, g * 2048:0]
                           if False else
                           vd[0:L, g * 128:(g + 1) * 128].rearrange("(a p) e -> p (a e)", p=128),
                           s_io, wait=[(s_dv, vv)])

                def fc2(e):
                    return e.tensor_copy(out=vf32[:], in_=vtmp[:])
                vv = P.op("vector", fc2, sem=s_dv, wait=[(s_io, v0)])
                P.dma("gpsimd", dbg[D + g * 128:D + (g + 1) * 128, :], vf32[:], s_io,
                      wait=[(s_dv, vv)])
            P.waits("gpsimd", [(s_io, s_io.n), (s_w, s_w.n)])
            _emit(nc, P, ctx)
            return nc

        att = cv(Cp, 0, 32 * 1024, BF16)
        khb = [cv(Ap, 0, 2048, BF16), cv(Ap, 4 * 1024, 2048, BF16)]
        vhb = [cv(Ap, 8 * 1024, 2048, BF16), cv(Ap, 12 * 1024, 2048, BF16)]
        scb = [cv(Ap, 16 * 1024, 2048, F32), cv(Ap, 24 * 1024, 2048, F32)]
        pbb = [cv(Ap, 32 * 1024, 2048, BF16), cv(Ap, 36 * 1024, 2048, BF16)]
        ptb = [cv(Ap, 40 * 1024, 2048, BF16), cv(Ap, 44 * 1024, 2048, BF16)]
        nm_t = [sb("nm0", [128, 1], F32), sb("nm1", [128, 1], F32)]
        sum_t = [sb("sum0", [128, 1], F32), sb("sum1", [128, 1], F32)]
        rc_t = [sb("rc0", [128, 1], F32), sb("rc1", [128, 1], F32)]
        ptps = PSB[:].bitcast(BF16)
        avv = PSB[:, 1536:1664]

        P.phase(PH_ATT)
        items = [(h, j) for h in range(H) for j in range(NBLK)]
        kh_v = [0, 0]
        vh_v = [0, 0]
        g_kh = [0, 0]
        g_vh = [0, 0]
        g_exp = [0, 0]
        g_tr = [0, 0]
        g_av = [0, 0]
        g_psa_a = 0
        g_psa_d = 0
        g_ptps = 0
        g_ae = 0

        kh_v[0] = P.dma("sync", khb[0][:], kTd[0:128, :], s_w)
        vh_v[0] = P.dma("sync", vhb[0][:].rearrange("p (a e) -> p a e", e=128),
                        vd.rearrange("(a p) e -> p a e", p=128)[:, :, 0:128], s_w)

        for idx, (h, j) in enumerate(items):
            ib = idx % 2
            hb = h % 2
            kext = (2 * j + 2) * 128
            nk = kext // 128
            if j == 0 and h + 1 < H:
                nb = (h + 1) % 2
                kh_v[nb] = P.dma("sync", khb[nb][:],
                                 kTd[(h + 1) * 128:(h + 2) * 128, :], s_w,
                                 wait=[(s_pe, g_kh[nb])])
                vh_v[nb] = P.dma("sync", vhb[nb][:].rearrange("p (a e) -> p a e", e=128),
                                 vd.rearrange("(a p) e -> p a e", p=128)[:, :,
                                     (h + 1) * 128:(h + 2) * 128], s_w,
                                 wait=[(s_pe, g_vh[nb])])

            def fsc_mm(e, h=h, j=j, hb=hb, kext=kext):
                last = None
                for c in range((kext + 511) // 512):
                    w = min(512, kext - c * 512)
                    last = e.matmul(PSA[:, c * 512:c * 512 + w],
                                    qres[:, h * 1024 + j * 128:h * 1024 + (j + 1) * 128],
                                    khb[hb][:, c * 512:c * 512 + w],
                                    start=True, stop=True)
                return last
            vscm = P.op("tensor", fsc_mm, sem=s_pe,
                        wait=[(s_w, kh_v[hb]), (s_ac, g_psa_a), (s_dv, g_psa_d)])
            g_kh[hb] = vscm

            if kext > 256:
                def fse(e, ib=ib, kext=kext):
                    return e.activation(out=scb[ib][:, 0:kext - 256],
                                        in_=PSA[:, 0:kext - 256], func=AF.Copy)
                vse = P.op("scalar", fse, sem=s_ac, wait=[(s_pe, vscm)])
            else:
                vse = s_ac.n
            g_psa_a = vse

            def fsm(e, ib=ib, j=j, kext=kext):
                return e.scalar_tensor_tensor(out=scb[ib][:, kext - 256:kext],
                                              in0=masks[:, j * 256:(j + 1) * 256],
                                              scalar=NEG, in1=PSA[:, kext - 256:kext],
                                              op0=ALU.mult, op1=ALU.add)
            vsm = P.op("vector", fsm, sem=s_dv,
                       wait=[(s_pe, vscm), (s_ac, g_exp[ib])])
            g_psa_d = vsm

            def fmx(e, ib=ib, kext=kext):
                return e.tensor_reduce(out=nm_t[ib][:], in_=scb[ib][:, 0:kext],
                                       axis=mybir.AxisListType.X, op=ALU.max, negate=True)
            vmx = P.op("vector", fmx, sem=s_dv, wait=[(s_ac, vse), (s_ac, g_exp[ib])])

            def fex(e, ib=ib, kext=kext):
                return e.activation(out=pbb[ib][:, 0:kext], in_=scb[ib][:, 0:kext],
                                    func=AF.Exp, bias=nm_t[ib][:, 0:1],
                                    accum_out=sum_t[ib][:, 0:1])
            vex = P.op("scalar", fex, sem=s_ac, wait=[(s_dv, vmx), (s_pe, g_tr[ib])])
            g_exp[ib] = vex

            def frc(e, ib=ib):
                return e.reciprocal(out=rc_t[ib][:], in_=sum_t[ib][:])
            P.op("vector", frc, sem=s_dv, wait=[(s_ac, vex)])

            def fnr(e, ib=ib, kext=kext):
                return e.tensor_scalar(out=pbb[ib][:, 0:kext], in0=pbb[ib][:, 0:kext],
                                       scalar1=rc_t[ib][:, 0:1], scalar2=None, op0=ALU.mult)
            vnr = P.op("vector", fnr, sem=s_dv)

            vpt_e = 0
            for half in range((nk + 7) // 8):
                nkk = min(8, nk - half * 8)

                def ftr2(e, ib=ib, half=half, nkk=nkk):
                    last = None
                    for kb in range(nkk):
                        last = e.transpose(ptps[:, kb * 128:(kb + 1) * 128],
                                           pbb[ib][:, (half * 8 + kb) * 128:
                                                   (half * 8 + kb + 1) * 128], ident_b[:])
                    return last
                vtr2 = P.op("tensor", ftr2, sem=s_pe,
                            wait=[(s_dv, vnr), (s_ac, g_ptps)])

                def fpe(e, ib=ib, half=half, nkk=nkk):
                    return e.activation(out=ptb[ib][:, half * 1024:half * 1024 + nkk * 128],
                                        in_=ptps[:, 0:nkk * 128], func=AF.Copy)
                vpt_e = P.op("scalar", fpe, sem=s_ac,
                             wait=[(s_pe, vtr2), (s_pe, g_av[ib])])
                g_ptps = vpt_e
            g_tr[ib] = s_pe.n

            def fav(e, ib=ib, hb=hb, nk=nk):
                last = None
                for kb in range(nk):
                    last = e.matmul(avv,
                                    r3(vhb[hb][:], 128)[:, kb, :],
                                    ptb[ib][:, kb * 128:(kb + 1) * 128],
                                    start=(kb == 0), stop=(kb == nk - 1))
                return last
            vav = P.op("tensor", fav, sem=s_pe,
                       wait=[(s_ac, vpt_e), (s_w, vh_v[hb]), (s_ac, g_ae)])
            g_vh[hb] = vav
            g_av[ib] = vav

            def fae(e, h=h, j=j):
                return e.activation(out=att[:, h * 1024 + j * 128:h * 1024 + (j + 1) * 128],
                                    in_=avv, func=AF.Identity, bias=bv_t[:, h:h + 1],
                                    scale=1.0)
            g_ae = P.op("scalar", fae, sem=s_ac, wait=[(s_pe, vav)])

        P.barrier([s_dv, s_ac, s_pe, s_io, s_w, s_sc])

        P.phase(PH_OPROJ)
        wof = cv(Ap, 0, 32 * 128, F32)
        wob = [cv(Ap, 16 * 1024, 32 * 128, BF16), cv(Ap, 24 * 1024, 32 * 128, BF16)]
        oev = [cv(Ap, 32 * 1024, 512, F32), cv(Ap, 34 * 1024, 512, F32)]
        xrr = [cv(Ap, 36 * 1024, 512, F32), cv(Ap, 38 * 1024, 512, F32)]
        xo2 = [cv(Ap, 40 * 1024, 512, F32), cv(Ap, 42 * 1024, 512, F32)]
        io_o = [0, 0]
        dv_w = [0, 0]
        for fc in range(32):
            bi = fc % 2
            vwf = P.dma("sync", wof[:], woi[:, fc], s_w, wait=[(s_dv, s_dv.n)])

            def fcw(e, bi=bi):
                return e.tensor_copy(out=wob[bi][:], in_=wof[:])
            vcw = P.op("vector", fcw, sem=s_dv, wait=[(s_w, vwf), (s_pe, s_pe.n)])
            dv_w[bi] = vcw

            for hf in range(2):
                def fom(e, bi=bi, hf=hf):
                    last = None
                    for ge in range(32):
                        last = e.matmul(a_[hf], wob[bi][:, ge * 128:(ge + 1) * 128],
                                        att[:, ge * 1024 + hf * 512:ge * 1024 + (hf + 1) * 512],
                                        start=(ge == 0), stop=(ge == 31))
                    return last
                vom = P.op("tensor", fom, sem=s_pe, wait=[(s_dv, vcw), (s_ac, s_ac.n)])

                def foe(e, fc=fc, hf=hf):
                    return e.activation(out=oev[hf][:], in_=a_[hf], func=AF.Identity,
                                        bias=bo_t[:, fc:fc + 1], scale=1.0)
                voe = P.op("scalar", foe, sem=s_ac, wait=[(s_pe, vom), (s_dv, s_dv.n)])

                xv = r3(xmid[fc * 128:(fc + 1) * 128, hf * 1024:(hf + 1) * 1024], 128)
                vx = P.dma("gpsimd", r3(xrr[hf][:], 128), xv[:, 0:8:2, :], s_io,
                           wait=[(s_dv, s_dv.n)])

                def fxa(e, hf=hf):
                    return e.tensor_tensor(out=xo2[hf][:], in0=oev[hf][:], in1=xrr[hf][:],
                                           op=ALU.add)
                vxa = P.op("vector", fxa, sem=s_dv,
                           wait=[(s_ac, voe), (s_io, vx), (s_io, io_o[hf])])
                vxa2 = fence_v()
                io_o[hf] = P.dma("gpsimd",
                                 xmid2[fc * 128:(fc + 1) * 128, hf * 512:(hf + 1) * 512],
                                 xo2[hf][:], s_io, wait=[(s_dv, vxa2)])
        P.barrier([s_dv, s_ac, s_pe, s_io, s_w, s_sc])

        if dump == "xmid2":
            vtmp2 = cv(Ap, 0, 1024, F32)
            for g in range(NG_D):
                v0 = P.dma("gpsimd", vtmp2[:], xmid2[g * 128:(g + 1) * 128, :], s_io)
                P.dma("gpsimd", dbg[g * 128:(g + 1) * 128, :], vtmp2[:], s_io,
                      wait=[(s_io, v0)])
            P.waits("gpsimd", [(s_io, s_io.n), (s_w, s_w.n)])
            _emit(nc, P, ctx)
            return nc

        for t in range(2):
            ln_phase(xmid2, t * 512, lnsl, xg_out=xg_t[0:NG_D, :])
            P.barrier([s_dv, s_ac, s_pe, s_io])
            gate_up(first_tile=False)
            P.barrier([s_dv, s_ac, s_pe, s_w, s_io])
            down(False, xmid2, t * 512, oT, t * 512)
            P.barrier([s_dv, s_ac, s_pe, s_w, s_io, s_sc, s_gp])

        P.waits("gpsimd", [(s_io, s_io.n), (s_w, s_w.n), (s_sc, s_sc.n)])
        _emit(nc, P, ctx)
    return nc


def _emit(nc, P, ctx):
    block = ctx.enter_context(nc.Block())
    for name in ENGINES:
        deco = getattr(block, name)

        def body(e, ops=P.ops[name]):
            for f in ops:
                f(e)
        deco(body)


def _stage_gu(qw):
    by = np.ascontiguousarray(qw).view(np.uint8).reshape(D, I // 2)[PI_D]
    return np.ascontiguousarray(by.reshape(NG_D, 128, NS2, 128).transpose(1, 2, 0, 3))


def _stage_dq(qw):
    by = np.ascontiguousarray(qw).view(np.uint8).reshape(I, D // 2)[PI_I]
    return np.ascontiguousarray(by.reshape(NG_I, 128, D // 2).transpose(1, 0, 2))


def _stage_wT(W, perm_rows, perm_cols):
    wT = np.asarray(W).T
    if perm_rows:
        wT = wT[PI_D]
    if perm_cols:
        wT = wT[:, PI_D]
    return np.ascontiguousarray(
        wT.reshape(32, 128, 32, 128).transpose(1, 2, 0, 3), dtype=np.float32)


def _stage_wv(W):
    wT = np.asarray(W).T[PI_D]
    return np.ascontiguousarray(wT.reshape(NG_D, 128, D).transpose(1, 0, 2),
                                dtype=np.float32)


def _stage_b(b, perm):
    bb = np.asarray(b)[PI_D] if perm else np.asarray(b)
    return np.ascontiguousarray(bb.reshape(32, 128).T, dtype=np.float32)


def _qz_bytes(qz):
    return np.ascontiguousarray(np.ascontiguousarray(qz).view(np.uint8).reshape(
        qz.shape[0], -1))


def stage_core(c, x, w):
    b, p = c // 2, c % 2
    blocks = np.arange(16)
    if p:
        blocks = blocks.reshape(8, 2)[:, ::-1].reshape(-1)
    cols = (blocks[:, None] * 128 + np.arange(128)[None, :]).reshape(-1)
    m = {
        "xT": np.ascontiguousarray(np.asarray(x[b]).T[PI_D][:, cols], dtype=np.float32),
        "qpos": np.ascontiguousarray(
            cols.reshape(16, 128)[0::2].T, dtype=np.float32),
        "kpos": np.ascontiguousarray(cols[None, :], dtype=np.float32),
    }
    m.update(w)
    return m, cols


def kernel(**inp):
    x = np.asarray(inp["L0_attn_x"], dtype=np.float32)
    w = {
        "qg": _stage_gu(inp["L0_ffn_gate_qweight"]),
        "qu": _stage_gu(inp["L0_ffn_up_qweight"]),
        "sg": np.ascontiguousarray(np.asarray(inp["L0_ffn_gate_scales"],
                                              dtype=np.float32)[:, PI_I]),
        "su": np.ascontiguousarray(np.asarray(inp["L0_ffn_up_scales"],
                                              dtype=np.float32)[:, PI_I]),
        "zgq": _qz_bytes(inp["L0_ffn_gate_qzeros"]),
        "zuq": _qz_bytes(inp["L0_ffn_up_qzeros"]),
        "qd": _stage_dq(inp["L0_ffn_down_qweight"]),
        "sd": np.ascontiguousarray(np.asarray(inp["L0_ffn_down_scales"],
                                              dtype=np.float32)[:, PI_D]),
        "zdq": _qz_bytes(inp["L0_ffn_down_qzeros"]),
        "wq": _stage_wT(inp["L0_attn_Wq"], True, False),
        "wk": _stage_wT(inp["L0_attn_Wk"], True, False),
        "wo": _stage_wT(inp["L0_attn_Wo"].T if False else inp["L0_attn_Wo"],
                        False, True),
        "wv": _stage_wv(inp["L0_attn_Wv"]),
        "bq": _stage_b(inp["L0_attn_bq"], False),
        "bk": _stage_b(inp["L0_attn_bk"], False),
        "bv": _stage_b(inp["L0_attn_bv"], False),
        "bo": _stage_b(inp["L0_attn_bo"], True),
    }
    nc = build_program()
    in_maps = []
    colmaps = []
    for c in range(8):
        m, cols = stage_core(c, x, w)
        in_maps.append(m)
        colmaps.append(cols)
    res = run_bass_kernel_spmd(nc, in_maps, core_ids=list(range(8)))

    out = np.empty((4, L, D), dtype=np.float32)
    inv = np.empty(D, dtype=np.int64)
    inv[PI_D] = np.arange(D)
    for c in range(8):
        b = c // 2
        own = colmaps[c].reshape(16, 128)[0::2].reshape(-1)
        oTc = np.asarray(res.results[c]["oT"])
        out[b, own, :] = oTc[inv].T
    return out


# version 04 (final)
# speedup vs baseline: 1.3335x; 1.0402x over previous
import os
import numpy as np
import ml_dtypes
from contextlib import ExitStack

import concourse.bass as bass
import concourse.mybir as mybir
from concourse.bass_utils import run_bass_kernel_spmd

F32 = mybir.dt.float32
BF16 = mybir.dt.bfloat16
U8 = mybir.dt.uint8
AF = mybir.ActivationFunctionType
ALU = mybir.AluOpType
BF = ml_dtypes.bfloat16

D = 4096
I = 11008
H = 32
L = 2048
OWN = 1024
NG_D = 32
NG_I = 86
NS2 = 43
NBLK = 8
EPS = 1e-5
QSCL = float(1.0 / np.sqrt(128.0))
NEG = -1.0e9

ENGINES = ("sync", "tensor", "vector", "scalar", "gpsimd")


def _pi(n):
    p = np.arange(n)
    blk, k = p // 128, p % 128
    return np.where(k < 64, blk * 128 + 2 * k, blk * 128 + 2 * (k - 64) + 1)


PI_D = _pi(D)
PI_I = _pi(I)


class Sem:
    def __init__(self, ctx, nc, name):
        self.h = ctx.enter_context(nc.semaphore(name))
        self.n = 0


SER_MASK = int(os.environ.get("KSER_MASK", "424"))
PH_INIT, PH_DQG, PH_DQD, PH_LN, PH_GU, PH_DOWN, PH_KQV, PH_ATT, PH_OPROJ = range(9)


class Prog:
    def __init__(self):
        self.ops = {e: [] for e in ENGINES}
        self.all_sems = []
        self.eng_sem = {}
        self.serial = True

    def phase(self, ph):
        self.serial = bool((SER_MASK >> ph) & 1)

    def _serial_wait(self):
        return [self.last] if getattr(self, "last", None) else []

    def emit(self, eng, fn):
        self.ops[eng].append(fn)

    def dma(self, q, out, in_, sem, wait=()):
        w = tuple(self._serial_wait()) + tuple(wait) if self.serial else tuple(wait)
        self.last = (sem, sem.n + 16)

        def f(e, out=out, in_=in_, w=w, sem=sem):
            for (s, v) in w:
                e.wait_ge(s.h, v)
            e.dma_start(out=out, in_=in_).then_inc(sem.h, 16)

        self.emit(q, f)
        sem.n += 16
        return sem.n

    def op(self, eng, fn, sem=None, inc=1, wait=()):
        if self.serial:
            wait = tuple(self._serial_wait()) + tuple(wait)
            if sem is None:
                sem = self.eng_sem[eng]
        if sem is not None:
            self.last = (sem, sem.n + inc)
        w = tuple(wait)

        def f(e, fn=fn, w=w, sem=sem, inc=inc):
            for (s, vv) in w:
                e.wait_ge(s.h, vv)
            ins = fn(e)
            if sem is not None:
                ins.then_inc(sem.h, inc)

        self.emit(eng, f)
        if sem is not None:
            sem.n += inc
            return sem.n
        return None

    def waits(self, eng, wait):
        w = tuple(wait)

        def f(e, w=w):
            for (s, v) in w:
                e.wait_ge(s.h, v)

        self.emit(eng, f)

    def barrier(self, sems):
        w = [(s, s.n) for s in sems]
        for e in ENGINES:
            self.waits(e, w)


def r3(ap, k):
    return ap.rearrange("p (a b) -> p a b", b=k)


def build_program(dump=None):
    nc = bass.Bass("TRN2", target_bir_lowering=False)

    din = lambda n, s, t: nc.dram_tensor(n, s, t, kind="ExternalInput").ap()
    xT = din("xT", [D, L], F32)
    qgi = din("qg", [128, NS2, NG_D, 128], U8)
    qui = din("qu", [128, NS2, NG_D, 128], U8)
    sgi = din("sg", [NG_D, I], F32)
    sui = din("su", [NG_D, I], F32)
    zgq = din("zgq", [NG_D, I // 2], U8)
    zuq = din("zuq", [NG_D, I // 2], U8)
    qdi = din("qd", [128, NG_I, D // 2], U8)
    sdi = din("sd", [NG_I, D], F32)
    zdq = din("zdq", [NG_I, D // 2], U8)
    wqi = din("wq", [128, 32, NG_D, 128], F32)
    wki = din("wk", [128, 32, NG_D, 128], F32)
    woi = din("wo", [128, 32, 32, 128], F32)
    wvi = din("wv", [128, NG_D, D], F32)
    bqi = din("bq", [128, 32], F32)
    bki = din("bk", [128, 32], F32)
    bvi = din("bv", [128, 32], F32)
    boi = din("bo", [128, 32], F32)
    qposi = din("qpos", [128, NBLK], F32)
    kposi = din("kpos", [1, L], F32)

    oT = nc.dram_tensor("oT", [D, OWN], F32, kind="ExternalOutput").ap()
    dbg = None
    if dump == "xmid":
        dbg = nc.dram_tensor("dbg", [D, L], F32, kind="ExternalOutput").ap()
    elif dump == "xmid2":
        dbg = nc.dram_tensor("dbg", [D, OWN], F32, kind="ExternalOutput").ap()
    elif dump == "kv":
        dbg = nc.dram_tensor("dbg", [2 * D, L], F32, kind="ExternalOutput").ap()
    elif dump == "ln0":
        dbg = nc.dram_tensor("dbg", [128, NG_D * 512], BF16, kind="ExternalOutput").ap()
        dbg2 = nc.dram_tensor("dbg2", [NG_D, 512], BF16, kind="ExternalOutput").ap()
        dbg3 = nc.dram_tensor("dbg3", [1, 2048], F32, kind="ExternalOutput").ap()
    elif dump == "h0":
        dbg = nc.dram_tensor("dbg", [128, NG_I * 512], BF16, kind="ExternalOutput").ap()
        dbg2 = nc.dram_tensor("dbg2", [NG_I, 512], BF16, kind="ExternalOutput").ap()
    elif dump == "od":
        dbg = nc.dram_tensor("dbg", [128, 8 * 4 * 512], BF16, kind="ExternalOutput").ap()
    elif dump == "dq":
        dbg = nc.dram_tensor("dbg", [128, 8, NG_D, 128], BF16, kind="ExternalOutput").ap()
        dbg2 = nc.dram_tensor("dbg2", [128, 2, D], BF16, kind="ExternalOutput").ap()
        dbg3 = nc.dram_tensor("dbg3", [NG_D, 256], BF16, kind="ExternalOutput").ap()

    cg = nc.dram_tensor("cg", [128, NG_I, NG_D, 128], BF16, kind="Internal").ap()
    cu = nc.dram_tensor("cu", [128, NG_I, NG_D, 128], BF16, kind="Internal").ap()
    cd = nc.dram_tensor("cd", [128, NG_I, D], BF16, kind="Internal").ap()
    zs_g = nc.dram_tensor("zs_g", [NG_D, I], BF16, kind="Internal").ap()
    zs_u = nc.dram_tensor("zs_u", [NG_D, I], BF16, kind="Internal").ap()
    zs_d = nc.dram_tensor("zs_d", [NG_I, D], BF16, kind="Internal").ap()
    xmid = nc.dram_tensor("xmid", [D, L], F32, kind="Internal").ap()
    xmid2 = nc.dram_tensor("xmid2", [D, OWN], F32, kind="Internal").ap()
    kTd = nc.dram_tensor("kTd", [D, L], BF16, kind="Internal").ap()
    vd = nc.dram_tensor("vd", [L, D], BF16, kind="Internal").ap()

    P = Prog()

    with ExitStack() as ctx:
        sb = lambda n, s, t: ctx.enter_context(nc.sbuf_tensor(n, s, t))

        s_w = Sem(ctx, nc, "s_w")
        s_io = Sem(ctx, nc, "s_io")
        s_sc = Sem(ctx, nc, "s_sc")
        s_pe = Sem(ctx, nc, "s_pe")
        s_dv = Sem(ctx, nc, "s_dv")
        s_ac = Sem(ctx, nc, "s_ac")
        s_gp = Sem(ctx, nc, "s_gp")
        P.all_sems = [s_w, s_io, s_sc, s_pe, s_dv, s_ac, s_gp]
        P.eng_sem = {"tensor": s_pe, "vector": s_dv, "scalar": s_ac,
                     "gpsimd": s_gp, "sync": s_w}

        Ap = sb("poolA", [128, 48 * 1024], U8)
        Bp = sb("poolB", [128, 64 * 1024], U8)
        Cp = sb("poolC", [128, 87 * 1024], U8)
        ones_f = sb("ones_f", [1, 128], F32)
        ones_b = sb("ones_b", [128, 1], BF16)
        iota_p = sb("iota_p", [128, 1], F32)
        iota256 = sb("iota256", [128, 256], F32)
        ident_b = sb("ident_b", [128, 128], BF16)
        bq_t = sb("bq_t", [128, 32], F32)
        bk_t = sb("bk_t", [128, 32], F32)
        bv_t = sb("bv_t", [128, 32], F32)
        bo_t = sb("bo_t", [128, 32], F32)
        qpos_t = sb("qpos_t", [128, NBLK], F32)
        masks = sb("masks", [128, NBLK * 256], BF16)
        cvals = sb("cvals", [1, 8], F32)
        CONSTS = (EPS, 1.0 / D, -0.5, QSCL, 1.0, 0.5)
        for _i, _v in enumerate(CONSTS):
            nc.const_aps.register(F32, _v, cvals[0:1, _i:_i + 1])
        gs32 = sb("gs32", [128, 2 * NG_D - 1], BF16)
        gs86 = sb("gs86", [128, 2 * NG_I - 1], BF16)

        fen_v = sb("fen_v", [1, 2], F32)
        fen_a = sb("fen_a", [1, 2], F32)

        def fence_v(wait=()):
            return P.op("vector", lambda e: e.memset(fen_v[0:1, :], 0.0), sem=s_dv,
                        wait=wait)

        def fence_a(wait=()):
            return P.op("scalar", lambda e: e.memzero(fen_a[0:1, :]), sem=s_ac,
                        wait=wait)

        PSA = ctx.enter_context(nc.psum_tensor("PSA", [128, 2048], F32))
        PSB = ctx.enter_context(nc.psum_tensor("PSB", [128, 2048], F32))
        a_ = [PSA[:, i * 512:(i + 1) * 512] for i in range(4)]
        b_ = [PSB[:, i * 512:(i + 1) * 512] for i in range(4)]

        def cv(pool, off, cols, dt):
            return pool[:, off:off + cols * mybir.dt.size(dt)].bitcast(dt)

        def g32sl(g):
            return gs32[:, NG_D - 1 - g:2 * NG_D - 1 - g]

        def g86sl(n):
            return gs86[:, NG_I - 1 - n:2 * NG_I - 1 - n]

        P.phase(PH_INIT)

        def f_cvals(e):
            last = None
            for _i, _v in enumerate(CONSTS):
                last = e.memset(cvals[0:1, _i:_i + 1], _v)
            return last
        P.op("vector", f_cvals, sem=s_dv)

        def f_iota(e):
            e.iota(iota_p[:], pattern=[[0, 1]], base=0, channel_multiplier=1,
                   allow_small_or_imprecise_dtypes=True)
            return e.iota(iota256[:], pattern=[[1, 256]], base=0, channel_multiplier=0,
                          allow_small_or_imprecise_dtypes=True)
        P.op("gpsimd", f_iota, sem=s_gp)

        kpos_t = cv(Bp, 56 * 1024, L, F32)[0:1, :]
        for t, srci in ((bq_t[:], bqi), (bk_t[:], bki), (bv_t[:], bvi), (bo_t[:], boi),
                        (qpos_t[:], qposi), (kpos_t, kposi)):
            P.dma("sync", t, srci, s_w)
        v_bias = s_w.n

        def f_const(e):
            e.memset(ones_f[:], 1.0)
            e.memset(ones_b[:], 1.0)
            e.memset(gs32[:], 0.0)
            e.memset(gs86[:], 0.0)
            e.memset(gs32[:, NG_D - 1:NG_D], -1.0)
            e.memset(gs86[:, NG_I - 1:NG_I], -1.0)
            return e.tensor_scalar(out=ident_b[:], in0=iota256[:, 0:128],
                                   scalar1=iota_p[:, 0:1], scalar2=None, op0=ALU.is_equal)
        P.op("vector", f_const, sem=s_dv, wait=[(s_gp, 1)])

        def f_krep(e):
            last = None
            for j in range(NBLK):
                last = e.matmul(PSA[:, j * 256:(j + 1) * 256], ones_f[:],
                                kpos_t[0:1, 2 * j * 128:2 * j * 128 + 256],
                                start=True, stop=True)
            return last
        vkr = P.op("tensor", f_krep, sem=s_pe, wait=[(s_w, v_bias), (s_dv, 1)])

        def f_masks(e):
            last = None
            for j in range(NBLK):
                last = e.tensor_scalar(
                    out=masks[:, j * 256:(j + 1) * 256],
                    in0=PSA[:, j * 256:(j + 1) * 256],
                    scalar1=qpos_t[:, j:j + 1], scalar2=None, op0=ALU.is_gt)
            return last
        P.op("vector", f_masks, sem=s_dv, wait=[(s_pe, vkr)])

        def f_bq(e):
            return e.tensor_scalar(out=bq_t[:], in0=bq_t[:], scalar1=QSCL, scalar2=None,
                                   op0=ALU.mult)
        P.op("vector", f_bq, sem=s_dv, wait=[(s_w, v_bias)])

        zrow = cv(Ap, 0, I // 2, U8)
        zlo = cv(Ap, 8 * 1024, I // 2, U8)
        zhi = cv(Ap, 16 * 1024, I // 2, U8)
        sful = cv(Bp, 0, I, F32)
        zout = cv(Cp, 0, I, BF16)

        def zbuild(qz, s_ap, zdram, npart, ncol):
            half = ncol // 2
            guard = [(s_dv, s_dv.n)]
            v0 = P.dma("gpsimd", zrow[0:npart, 0:half], qz, s_io, wait=guard)
            v1 = P.dma("sync", sful[0:npart, 0:ncol], s_ap, s_w, wait=guard)

            def f(e):
                e.tensor_scalar(out=zlo[0:npart, 0:half], in0=zrow[0:npart, 0:half],
                                scalar1=15, scalar2=None, op0=ALU.bitwise_and)
                e.tensor_scalar(out=zhi[0:npart, 0:half], in0=zrow[0:npart, 0:half],
                                scalar1=4, scalar2=None, op0=ALU.logical_shift_right)
                z3 = r3(zout[0:npart, 0:ncol], 128)
                s3 = r3(sful[0:npart, 0:ncol], 128)
                l3 = r3(zlo[0:npart, 0:half], 64)
                h3 = r3(zhi[0:npart, 0:half], 64)
                e.tensor_tensor(out=z3[:, :, 0:64], in0=l3, in1=s3[:, :, 0:64], op=ALU.mult)
                return e.tensor_tensor(out=z3[:, :, 64:128], in0=h3, in1=s3[:, :, 64:128],
                                       op=ALU.mult)
            P.op("vector", f, sem=s_dv, wait=[(s_io, v0), (s_w, v1)])
            vv = fence_v()
            return P.dma("gpsimd", zdram, zout[0:npart, 0:ncol], s_io, wait=[(s_dv, vv)])

        zbuild(zgq, sgi, zs_g, NG_D, I)
        zbuild(zuq, sui, zs_u, NG_D, I)
        zbuild(zdq, sdi, zs_d, NG_I, D)
        P.barrier([s_io, s_dv, s_w, s_gp, s_pe])

        cw_gate = [0] * NG_I
        cw_up = [0] * NG_I
        cw_down = [0] * NG_I
        qb2 = [cv(Ap, 0, 4096, U8), cv(Ap, 4 * 1024, 4096, U8)]
        ulo = cv(Ap, 8 * 1024, 4096, U8)
        uhi = cv(Ap, 12 * 1024, 4096, U8)
        dqs = [cv(Bp, 0, 32 * 256, BF16), cv(Bp, 16 * 1024, 32 * 256, BF16),
               cv(Bp, 32 * 1024, 32 * 256, BF16), cv(Bp, 48 * 1024, 32 * 256, BF16)]
        srw = [cv(Ap, 16 * 1024, 256, F32)[0:1, :], cv(Ap, 17 * 1024, 256, F32)[0:1, :]]

        def dq_matrix(qsrc, ssrc, cdst, cwlist, s0, s1, qbufs, slabbufs):
            dv_after = [0] * NS2
            for s in range(NS2):
                bi = s % 2
                vq = P.dma("gpsimd", qbufs[bi][:], qsrc[:, s], s_io,
                           wait=[(s_dv, dv_after[s - 2])] if s >= 2 else [])

                def fum(e, qb=qbufs[bi]):
                    e.tensor_scalar(out=ulo[:], in0=qb[:], scalar1=15, scalar2=None,
                                    op0=ALU.bitwise_and)
                    return e.tensor_scalar(out=uhi[:], in0=qb[:], scalar1=4,
                                           scalar2=None, op0=ALU.logical_shift_right)
                vum = P.op("vector", fum, sem=s_dv, wait=[(s_io, vq)])

                for g in range(NG_D):
                    sbi = g % 2
                    vsr = P.dma("sync", srw[sbi][0:1, :],
                                ssrc[g:g + 1, s * 256:(s + 1) * 256], s_sc,
                                wait=[(s_pe, s_pe.n)])

                    def frp(e, sbi=sbi, s0=s0, s1=s1):
                        return e.matmul((s0 if sbi == 0 else s1)[:, 0:256], ones_f[:],
                                        srw[sbi][0:1, :], start=True, stop=True)
                    vrp = P.op("tensor", frp, sem=s_pe, wait=[(s_sc, vsr), (s_dv, s_dv.n)])

                    def fsc(e, g=g, sbi=sbi, s0=s0, s1=s1, slab=slabbufs[bi]):
                        o4 = r3(slab[:], 32 * 128).rearrange("p a (g k) -> p a g k", k=128)
                        l3 = r3(ulo[:, g * 128:(g + 1) * 128], 64)
                        h3 = r3(uhi[:, g * 128:(g + 1) * 128], 64)
                        sr3 = r3((s0 if sbi == 0 else s1)[:, 0:256], 128)
                        e.tensor_tensor(out=o4[:, :, g, 0:64], in0=l3, in1=sr3[:, :, 0:64],
                                        op=ALU.mult)
                        return e.tensor_tensor(out=o4[:, :, g, 64:128], in0=h3,
                                               in1=sr3[:, :, 64:128], op=ALU.mult)
                    wfsc = [(s_pe, vrp), (s_dv, vum)]
                    if g == 0 and s >= 2:
                        wfsc.append((s_io, cwlist[2 * (s - 2)]))
                    P.op("vector", fsc, sem=s_dv, wait=wfsc)
                dv_after[s] = fence_v()
                vcw = P.dma("gpsimd", cdst[:, 2 * s:2 * s + 2],
                            r3(slabbufs[bi][:], 32 * 128), s_io, wait=[(s_dv, dv_after[s])])
                cwlist[2 * s] = vcw
                cwlist[2 * s + 1] = vcw

        P.phase(PH_DQG)
        dq_matrix(qgi, sgi, cg, cw_gate, b_[0], b_[1], qb2, dqs[0:2])
        dq_matrix(qui, sui, cu, cw_up, b_[0], b_[1], qb2[::-1], dqs[2:4])

        qrow = [cv(Ap, 0, 2048, U8), cv(Ap, 2 * 1024, 2048, U8)]
        dlo = cv(Ap, 4 * 1024, 2048, U8)
        dhi = cv(Ap, 6 * 1024, 2048, U8)
        dout = [cv(Ap, 8 * 1024, D, BF16), cv(Ap, 16 * 1024, D, BF16)]
        srd = [cv(Bp, 0, D, F32)[0:1, :], cv(Bp, 16 * 1024, D, F32)[0:1, :]]
        dv_aft = [0] * NG_I
        P.phase(PH_DQD)
        v_gu_done = [(s_io, s_io.n), (s_dv, s_dv.n)]
        for g in range(NG_I):
            bi = g % 2
            vs = P.dma("sync", srd[bi][0:1, :], sdi[g:g + 1, :], s_sc,
                       wait=[(s_pe, s_pe.n)] + ([] if g >= 2 else v_gu_done))
            vq = P.dma("gpsimd", qrow[bi][:], qdi[:, g, :], s_io,
                       wait=[(s_dv, dv_aft[g - 2])] if g >= 2 else v_gu_done)

            def fum(e, bi=bi):
                e.tensor_scalar(out=dlo[:], in0=qrow[bi][:], scalar1=15, scalar2=None,
                                op0=ALU.bitwise_and)
                return e.tensor_scalar(out=dhi[:], in0=qrow[bi][:], scalar1=4,
                                       scalar2=None, op0=ALU.logical_shift_right)
            vum = P.op("vector", fum, sem=s_dv, wait=[(s_io, vq)])

            for c in range(4):
                def frp(e, bi=bi, c=c):
                    e.matmul(b_[0][:], ones_f[:], srd[bi][0:1, c * 1024:c * 1024 + 512],
                             start=True, stop=True)
                    return e.matmul(b_[1][:], ones_f[:],
                                    srd[bi][0:1, c * 1024 + 512:(c + 1) * 1024],
                                    start=True, stop=True)
                vrp = P.op("tensor", frp, sem=s_pe, wait=[(s_sc, vs), (s_dv, s_dv.n)])

                def ftt(e, bi=bi, c=c):
                    o3 = r3(dout[bi][:, c * 1024:(c + 1) * 1024], 128)
                    l3 = r3(dlo[:, c * 512:(c + 1) * 512], 64)
                    h3 = r3(dhi[:, c * 512:(c + 1) * 512], 64)
                    s0 = r3(b_[0], 128)
                    s1 = r3(b_[1], 128)
                    e.tensor_tensor(out=o3[:, 0:4, 0:64], in0=l3[:, 0:4], in1=s0[:, :, 0:64],
                                    op=ALU.mult)
                    e.tensor_tensor(out=o3[:, 0:4, 64:128], in0=h3[:, 0:4],
                                    in1=s0[:, :, 64:128], op=ALU.mult)
                    e.tensor_tensor(out=o3[:, 4:8, 0:64], in0=l3[:, 4:8], in1=s1[:, :, 0:64],
                                    op=ALU.mult)
                    return e.tensor_tensor(out=o3[:, 4:8, 64:128], in0=h3[:, 4:8],
                                           in1=s1[:, :, 64:128], op=ALU.mult)
                wftt = [(s_pe, vrp), (s_dv, vum)]
                if c == 0 and g >= 2:
                    wftt.append((s_io, cw_down[g - 2]))
                P.op("vector", ftt, sem=s_dv, wait=wftt)
            dv_aft[g] = fence_v()
            cw_down[g] = P.dma("gpsimd", cd[:, g, :], dout[bi][:], s_io,
                               wait=[(s_dv, dv_aft[g])])

        if dump == "dq":
            dqtmp = cv(Bp, 0, 8 * NG_D * 128, BF16)
            v0 = P.dma("gpsimd", dqtmp[:].rearrange("p (a g k) -> p a g k", g=NG_D, k=128),
                       cg[:, 78:86], s_io, wait=[(s_io, s_io.n)])
            P.dma("gpsimd", dbg, dqtmp[:].rearrange("p (a g k) -> p a g k", g=NG_D, k=128),
                  s_io, wait=[(s_io, v0)])
            dqt2 = cv(Bp, 32 * 1024, 2 * D, BF16)
            v0 = P.dma("gpsimd", dqt2[:].rearrange("p (a k) -> p a k", k=D), cd[:, 0:2], s_io)
            P.dma("gpsimd", dbg2, dqt2[:].rearrange("p (a k) -> p a k", k=D), s_io,
                  wait=[(s_io, v0)])
            dqt3 = cv(Bp, 40 * 1024, 256, BF16)
            v0 = P.dma("gpsimd", dqt3[0:NG_D, :], zs_g[:, 0:256], s_io)
            P.dma("gpsimd", dbg3, dqt3[0:NG_D, :], s_io, wait=[(s_io, v0)])
            P.waits("gpsimd", [(s_io, s_io.n), (s_w, s_w.n)])
            _emit(nc, P, ctx)
            return nc

        if int(os.environ.get("KDQBAR", "0")):
            P.barrier([s_dv, s_ac, s_pe, s_w, s_io, s_sc, s_gp])

        st_row = cv(Ap, 0, 2048, F32)[0:1, :]
        xck = [cv(Ap, 24 * 1024, 512, F32), cv(Ap, 26 * 1024, 512, F32)]
        x2s = [cv(Ap, 28 * 1024, 512, BF16), cv(Ap, 29 * 1024, 512, BF16)]

        def ln_phase(src, c0, ln_slot, xg_out=None):
            P.phase(PH_LN)
            ps_s1 = b_[0][0:1, :]
            ps_s2 = b_[1][0:1, :]
            ps_rep = b_[2]
            ps_rep2 = b_[3]
            ps_xgv = b_[0][0:NG_D, :]
            dvh = [s_dv.n, s_dv.n]
            for g in range(NG_D):
                bi = g % 2
                vx = P.dma("gpsimd", xck[bi][:], src[g * 128:(g + 1) * 128, c0:c0 + 512],
                           s_io, wait=[(s_dv, dvh[bi])])

                def fcv(e, g=g, bi=bi):
                    return e.tensor_copy(out=ln_slot(g), in_=xck[bi][:])
                dvh[bi] = P.op("vector", fcv, sem=s_dv, wait=[(s_io, vx)])

                def fsq(e, g=g, bi=bi):
                    return e.activation(out=x2s[bi][:], in_=ln_slot(g), func=AF.Square)
                vsq = P.op("scalar", fsq, sem=s_ac,
                           wait=[(s_dv, dvh[bi]), (s_pe, s_pe.n)])

                def fst(e, g=g, bi=bi):
                    e.matmul(ps_s1, ones_b[:], ln_slot(g),
                             start=(g == 0), stop=(g == NG_D - 1))
                    return e.matmul(ps_s2, ones_b[:], x2s[bi][:],
                                    start=(g == 0), stop=(g == NG_D - 1))
                P.op("tensor", fst, sem=s_pe, wait=[(s_ac, vsq)])
            v_stats = s_pe.n

            st_m = st_row[0:1, 0:512]
            st_v = st_row[0:1, 512:1024]
            st_r = st_row[0:1, 1024:1536]
            st_mr = st_row[0:1, 1536:2048]

            def fev(e):
                e.activation(out=st_m, in_=ps_s1, func=AF.Copy, scale=1.0 / D)
                return e.activation(out=st_v, in_=ps_s2, func=AF.Copy, scale=1.0 / D)
            vev = P.op("scalar", fev, sem=s_ac, wait=[(s_pe, v_stats)])

            def fvar(e):
                e.tensor_tensor(out=st_mr, in0=st_m, in1=st_m, op=ALU.mult)
                return e.tensor_tensor(out=st_v, in0=st_v, in1=st_mr, op=ALU.subtract)
            vva = P.op("vector", fvar, sem=s_dv, wait=[(s_ac, vev)])

            def frs(e):
                e.activation(out=st_r, in_=st_v, func=AF.Ln, bias=EPS)
                return e.activation(out=st_r, in_=st_r, func=AF.Exp, scale=-0.5)
            vrs = P.op("scalar", frs, sem=s_ac, wait=[(s_dv, vva)])

            def fmr(e):
                return e.tensor_tensor(out=st_mr, in0=st_m, in1=st_r, op=ALU.mult)
            vmr = P.op("vector", fmr, sem=s_dv, wait=[(s_ac, vrs)])

            def frep(e):
                e.matmul(ps_rep, ones_f[:], st_r, start=True, stop=True)
                return e.matmul(ps_rep2, ones_f[:], st_mr, start=True, stop=True)
            vrep = P.op("tensor", frep, sem=s_pe, wait=[(s_dv, vmr)])

            for g in range(NG_D):
                def fnm(e, g=g):
                    sl = ln_slot(g)
                    e.tensor_tensor(out=sl, in0=sl, in1=ps_rep, op=ALU.mult)
                    return e.tensor_tensor(out=sl, in0=sl, in1=ps_rep2, op=ALU.subtract)
                vn = P.op("vector", fnm, sem=s_dv, wait=[(s_pe, vrep)])
                if xg_out is not None:
                    def fxg(e, g=g):
                        return e.matmul(ps_xgv, g32sl(g), ln_slot(g),
                                        start=(g == 0), stop=(g == NG_D - 1))
                    P.op("tensor", fxg, sem=s_pe, wait=[(s_dv, vn)])
            if xg_out is not None:
                def fxe(e):
                    return e.activation(out=xg_out, in_=ps_xgv, func=AF.Copy)
                P.op("scalar", fxe, sem=s_ac, wait=[(s_pe, s_pe.n)])

        ln_t = cv(Bp, 0, NG_D * 512, BF16)
        lnsl = lambda g: ln_t[:, g * 512:(g + 1) * 512]
        gslab = [cv(Bp, 32 * 1024, 32 * 128, BF16), cv(Bp, 40 * 1024, 32 * 128, BF16)]
        uslab = [cv(Bp, 48 * 1024, 32 * 128, BF16), cv(Bp, 56 * 1024, 32 * 128, BF16)]
        hbuf = cv(Cp, 0, NG_I * 512, BF16)
        xg_t = cv(Ap, 30 * 1024, 512, BF16)
        zscg = [cv(Ap, 31 * 1024, 128, BF16), cv(Ap, 31 * 1024 + 256, 128, BF16)]
        zscu = [cv(Ap, 31 * 1024 + 512, 128, BF16), cv(Ap, 31 * 1024 + 768, 128, BF16)]
        hg_o = cv(Ap, 32 * 1024, 512, BF16)
        sil = [cv(Ap, 33 * 1024, 512, BF16), cv(Ap, 34 * 1024, 512, BF16)]
        wdb = [cv(Ap, 35 * 1024, 512, BF16), cv(Ap, 36 * 1024, 512, BF16),
               cv(Ap, 46 * 1024, 512, BF16)]
        zsdc = cv(Ap, 37 * 1024, 512, BF16)
        obf = cv(Ap, 38 * 1024, 4 * 512, BF16)
        xres = [cv(Ap, 42 * 1024, 512, F32), cv(Ap, 44 * 1024, 512, F32)]
        xoutb = [cv(Ap, 16 * 1024, 512, F32), cv(Ap, 18 * 1024, 512, F32)]

        def gate_up(first_tile):
            P.phase(PH_GU)
            ps_gv = [a_[0], a_[1]]
            ps_uv = [a_[2], a_[3]]
            ps_hgv = b_[0][0:NG_I, :]
            ev_g = [0, 0]
            ev_u = [0, 0]
            pe_aft = [0] * NG_I
            for n in range(NG_I):
                bi = n % 2
                wsl = [(s_pe, pe_aft[n - 2])] if n >= 2 else []
                if first_tile:
                    wsl = wsl + [(s_io, cw_gate[n])]
                P.dma("sync", gslab[bi][:], cg[:, n], s_w, wait=wsl)
                wsl2 = [(s_io, cw_up[n])] if first_tile else []
                P.dma("sync", uslab[bi][:], cu[:, n], s_w, wait=wsl2)
                P.dma("sync", zscg[bi][0:NG_D, :], zs_g[:, n * 128:(n + 1) * 128], s_w)
                vz = P.dma("sync", zscu[bi][0:NG_D, :], zs_u[:, n * 128:(n + 1) * 128], s_w)

                def fmm(e, n=n, bi=bi):
                    for g in range(NG_D):
                        e.matmul(ps_gv[bi], gslab[bi][:, g * 128:(g + 1) * 128],
                                 lnsl(g), start=(g == 0), stop=False)
                        e.matmul(ps_uv[bi], uslab[bi][:, g * 128:(g + 1) * 128],
                                 lnsl(g), start=(g == 0), stop=False)
                    e.matmul(ps_gv[bi], zscg[bi][0:NG_D, :], xg_t[0:NG_D, :],
                             start=False, stop=True)
                    return e.matmul(ps_uv[bi], zscu[bi][0:NG_D, :], xg_t[0:NG_D, :],
                                    start=False, stop=True)
                vmm = P.op("tensor", fmm, sem=s_pe,
                           wait=[(s_w, vz), (s_ac, ev_g[bi]), (s_dv, ev_u[bi])])
                pe_aft[n] = vmm

                def fsi(e, bi=bi):
                    return e.activation(out=sil[bi][:], in_=ps_gv[bi], func=AF.Silu)
                ev_g[bi] = P.op("scalar", fsi, sem=s_ac,
                                wait=[(s_pe, vmm), (s_dv, ev_u[bi])])

                def fh(e, n=n, bi=bi):
                    return e.tensor_tensor(out=hbuf[:, n * 512:(n + 1) * 512],
                                           in0=sil[bi][:], in1=ps_uv[bi], op=ALU.mult)
                ev_u[bi] = P.op("vector", fh, sem=s_dv, wait=[(s_ac, ev_g[bi])])

                def fhg(e, n=n):
                    return e.matmul(ps_hgv, g86sl(n),
                                    hbuf[:, n * 512:(n + 1) * 512],
                                    start=(n == 0), stop=(n == NG_I - 1))
                P.op("tensor", fhg, sem=s_pe, wait=[(s_dv, ev_u[bi])])

            def fhe(e):
                return e.activation(out=hg_o[0:NG_I, :], in_=b_[0][0:NG_I, :], func=AF.Copy)
            P.op("scalar", fhe, sem=s_ac, wait=[(s_pe, s_pe.n)])

        psb_bf = PSB[:].bitcast(BF16)

        DOWN_BARRIER = bool(int(os.environ.get("KDOWN_BARRIER", "1")))
        DOWN_TAIL = bool(int(os.environ.get("KDOWN_TAIL", "1")))
        DOWN_W1 = bool(int(os.environ.get("KDOWN_W1", "1")))

        def down(first_tile, x_src, c0s, x_dst, c0d, extra_dst=None, extra_c0=0):
            P.phase(PH_DOWN)
            ps_ov = [a_[0], a_[1], a_[2], a_[3]]
            ev = 0
            vst_l = 0
            io_x = [0, 0]
            pe_aft = [0] * NG_I
            for fs in range(8):
                if DOWN_BARRIER:
                    P.barrier([s_dv, s_ac, s_pe, s_w, s_io, s_sc])
                if DOWN_TAIL:
                    P.serial = False
                for g in range(NG_I):
                    bi = g % 3
                    wv_ = [(s_pe, s_pe.n if DOWN_W1 else pe_aft[g - 2])]
                    if first_tile and fs == 0:
                        wv_ = wv_ + [(s_io, cw_down[g])]
                    vw = P.dma("sync", wdb[bi][:], cd[:, g, fs * 512:(fs + 1) * 512],
                               s_w, wait=wv_)

                    def fdm(e, g=g, bi=bi):
                        last = None
                        for tc in range(4):
                            last = e.matmul(ps_ov[tc],
                                            hbuf[:, g * 512 + tc * 128:g * 512 + tc * 128 + 128],
                                            wdb[bi][:], start=(g == 0), stop=False)
                        return last
                    pe_aft[g] = P.op("tensor", fdm, sem=s_pe,
                                     wait=[(s_w, vw)] + ([(s_ac, ev)] if g == 0 else []))

                if DOWN_TAIL:
                    P.serial = True
                vzd = P.dma("sync", zsdc[0:NG_I, :], zs_d[:, fs * 512:(fs + 1) * 512], s_w,
                            wait=[(s_pe, pe_aft[NG_I - 2])])

                def fcr(e):
                    last = None
                    for tc in range(4):
                        last = e.matmul(ps_ov[tc], hg_o[0:NG_I, tc * 128:(tc + 1) * 128],
                                        zsdc[0:NG_I, :], start=False, stop=True)
                    return last
                vcr = P.op("tensor", fcr, sem=s_pe, wait=[(s_w, vzd)])

                v_tr_prev = s_pe.n
                for tc in range(4):
                    def fev(e, tc=tc):
                        return e.activation(out=obf[:, tc * 512:(tc + 1) * 512],
                                            in_=ps_ov[tc], func=AF.Copy)
                    ev = P.op("scalar", fev, sem=s_ac, wait=[(s_pe, vcr)])
                v_ev = ev
                ev = fence_a()
                if dump == "od" and first_tile and c0d == 0:
                    vfa = fence_a()
                    P.dma("gpsimd", dbg[:, fs * 2048:(fs + 1) * 2048], obf[:], s_io,
                          wait=[(s_ac, vfa)])

                for fsub in range(4):
                    fglob = fs * 4 + fsub
                    bi = fglob % 2

                    def ftr(e, fsub=fsub):
                        for tc in range(4):
                            e.transpose(
                                psb_bf[:, tc * 128:(tc + 1) * 128],
                                obf[:, tc * 512 + fsub * 128:tc * 512 + fsub * 128 + 128],
                                ident_b[:])
                        return e.transpose(
                            psb_bf[:, 3 * 128:4 * 128],
                            obf[:, 3 * 512 + fsub * 128:3 * 512 + fsub * 128 + 128],
                            ident_b[:])
                    vtr = P.op("tensor", ftr, sem=s_pe,
                               wait=[(s_ac, v_ev), (s_dv, vst_l)])

                    vx = P.dma("gpsimd", xres[bi][:],
                               x_src[fglob * 128:(fglob + 1) * 128, c0s:c0s + 512], s_io,
                               wait=[(s_dv, s_dv.n)] if fglob >= 2 else [])

                    def fst(e, bi=bi):
                        return e.scalar_tensor_tensor(
                            out=xoutb[bi][:], in0=psb_bf[:, 0:512],
                            scalar=0.5, in1=xres[bi][:], op0=ALU.mult, op1=ALU.add)
                    P.op("vector", fst, sem=s_dv,
                         wait=[(s_pe, vtr), (s_io, vx), (s_io, io_x[bi])])
                    vst_l = fence_v()
                    io_x[bi] = P.dma("gpsimd",
                                     x_dst[fglob * 128:(fglob + 1) * 128, c0d:c0d + 512],
                                     xoutb[bi][:], s_io, wait=[(s_dv, vst_l)])
                    if extra_dst is not None:
                        io_x[bi] = P.dma(
                            "gpsimd",
                            extra_dst[fglob * 128:(fglob + 1) * 128, extra_c0:extra_c0 + 512],
                            xoutb[bi][:], s_io, wait=[(s_dv, vst_l)])

        for t in range(4):
            ln_phase(xT, t * 512, lnsl, xg_out=xg_t[0:NG_D, :])
            P.barrier([s_dv, s_ac, s_pe, s_io])
            if t == 0 and dump == "ln0":
                P.dma("gpsimd", dbg, ln_t[:], s_io)
                P.dma("gpsimd", dbg2, xg_t[0:NG_D, :], s_io)
                P.dma("gpsimd", dbg3, st_row[0:1, :], s_io)
                P.waits("gpsimd", [(s_io, s_io.n), (s_w, s_w.n)])
                _emit(nc, P, ctx)
                return nc
            gate_up(first_tile=(t == 0))
            P.barrier([s_dv, s_ac, s_pe, s_w, s_io])
            if t == 0 and dump == "h0":
                P.dma("gpsimd", dbg, hbuf[:], s_io)
                P.dma("gpsimd", dbg2, hg_o[0:NG_I, :], s_io)
                P.waits("gpsimd", [(s_io, s_io.n), (s_w, s_w.n)])
                _emit(nc, P, ctx)
                return nc
            down(t == 0, xT, t * 512, xmid, t * 512,
                 extra_dst=dbg if dump == "xmid" else None, extra_c0=t * 512)
            P.barrier([s_dv, s_ac, s_pe, s_w, s_io, s_sc, s_gp])
            if dump == "od" and t == 0:
                P.waits("gpsimd", [(s_io, s_io.n), (s_w, s_w.n)])
                _emit(nc, P, ctx)
                return nc

        if dump == "xmid":
            P.waits("gpsimd", [(s_io, s_io.n), (s_w, s_w.n)])
            _emit(nc, P, ctx)
            return nc

        ln2C = cv(Cp, 0, NG_D * 1024, BF16)
        qres = cv(Bp, 0, 32 * 1024, BF16)
        wslf = cv(Ap, 0, 32 * 128, F32)
        wbfs = [cv(Ap, 16 * 1024, 32 * 128, BF16), cv(Ap, 24 * 1024, 32 * 128, BF16)]
        kob = [cv(Ap, 32 * 1024, 512, BF16), cv(Ap, 33 * 1024, 512, BF16)]
        vvf = [cv(Ap, 34 * 1024, 512, F32), cv(Ap, 36 * 1024, 512, F32)]
        vvb2 = [cv(Ap, 38 * 1024, 512, BF16), cv(Ap, 39 * 1024, 512, BF16)]
        vob = [cv(Ap, 40 * 1024, 512, BF16), cv(Ap, 41 * 1024, 512, BF16)]

        for hf in range(2):
            ln_phase(xmid, hf * 1024, lambda g, hf=hf: ln2C[:, g * 1024:g * 1024 + 512])
            ln_phase(xmid, hf * 1024 + 512,
                     lambda g, hf=hf: ln2C[:, g * 1024 + 512:(g + 1) * 1024])
            P.barrier([s_dv, s_ac, s_pe, s_io, s_w, s_sc])

            P.phase(PH_KQV)
            io_k = [0, 0]
            for mat, bias_t, isq in ((wki, bk_t, False), (wqi, bq_t, True)):
                for ec in range(32):
                    bi = ec % 2
                    vwf = P.dma("sync", wslf[:], mat[:, ec], s_w, wait=[(s_dv, s_dv.n)])

                    def fcv2(e, bi=bi):
                        return e.tensor_copy(out=wbfs[bi][:], in_=wslf[:])
                    vcv = P.op("vector", fcv2, sem=s_dv,
                               wait=[(s_w, vwf), (s_pe, s_pe.n)])

                    if not isq:
                        for sub in range(2):
                            def fkm(e, bi=bi, sub=sub):
                                last = None
                                for g in range(NG_D):
                                    last = e.matmul(a_[sub],
                                                    wbfs[bi][:, g * 128:(g + 1) * 128],
                                                    ln2C[:, g * 1024 + sub * 512:
                                                         g * 1024 + (sub + 1) * 512],
                                                    start=(g == 0), stop=(g == NG_D - 1))
                                return last
                            vkm = P.op("tensor", fkm, sem=s_pe,
                                       wait=[(s_dv, vcv), (s_ac, s_ac.n)])

                            def fke(e, ec=ec, sub=sub, bias_t=bias_t):
                                return e.activation(out=kob[sub][:], in_=a_[sub],
                                                    func=AF.Identity,
                                                    bias=bias_t[:, ec:ec + 1], scale=1.0)
                            P.op("scalar", fke, sem=s_ac,
                                 wait=[(s_pe, vkm), (s_io, io_k[sub])])
                            vke = fence_a()
                            io_k[sub] = P.dma(
                                "gpsimd",
                                kTd[ec * 128:(ec + 1) * 128,
                                    hf * 1024 + sub * 512:hf * 1024 + (sub + 1) * 512],
                                kob[sub][:], s_ac if False else s_io,
                                wait=[(s_ac, vke)])
                    else:
                        def fqm(e, bi=bi):
                            last = None
                            for g in range(NG_D):
                                rhs = r3(ln2C[:, g * 1024:(g + 1) * 1024], 128)[:, 0:8:2, :]
                                last = e.matmul(a_[2], wbfs[bi][:, g * 128:(g + 1) * 128],
                                                rhs, start=(g == 0), stop=(g == NG_D - 1))
                            return last
                        vqm = P.op("tensor", fqm, sem=s_pe,
                                   wait=[(s_dv, vcv), (s_ac, s_ac.n)])

                        def fqe(e, ec=ec, hf=hf):
                            return e.activation(out=qres[:, ec * 1024 + hf * 512:
                                                        ec * 1024 + (hf + 1) * 512],
                                                in_=a_[2], func=AF.Identity,
                                                bias=bq_t[:, ec:ec + 1], scale=QSCL)
                        P.op("scalar", fqe, sem=s_ac, wait=[(s_pe, vqm)])

            io_vob = [0, 0]
            for eq in range(8):
                vg_after = [0, 0]
                for g in range(NG_D):
                    bi = g % 2
                    vwf = P.dma("sync", vvf[bi][:], wvi[:, g, eq * 512:(eq + 1) * 512],
                                s_w, wait=[(s_pe, vg_after[bi])])

                    def fcv3(e, bi=bi):
                        return e.tensor_copy(out=vvb2[bi][:], in_=vvf[bi][:])
                    vcv = P.op("vector", fcv3, sem=s_dv,
                               wait=[(s_w, vwf), (s_pe, vg_after[bi])])

                    def fvm(e, g=g, bi=bi, hf=hf):
                        last = None
                        for tb in range(8):
                            last = e.matmul(
                                (a_ + b_)[tb],
                                ln2C[:, g * 1024 + tb * 128:g * 1024 + tb * 128 + 128],
                                vvb2[bi][:], start=(g == 0), stop=(g == NG_D - 1))
                        return last
                    vg_after[bi] = P.op("tensor", fvm, sem=s_pe,
                                        wait=[(s_dv, vcv)] +
                                        ([(s_ac, s_ac.n)] if g == 0 else []))
                vvm = s_pe.n
                for tb in range(8):
                    bi2 = tb % 2

                    def fve(e, tb=tb, bi2=bi2):
                        return e.activation(out=vob[bi2][:], in_=(a_ + b_)[tb], func=AF.Copy)
                    P.op("scalar", fve, sem=s_ac,
                         wait=[(s_pe, vvm), (s_io, io_vob[bi2])])
                    vve = fence_a()
                    io_vob[bi2] = P.dma(
                        "gpsimd",
                        vd[hf * 1024 + tb * 128:hf * 1024 + (tb + 1) * 128,
                           eq * 512:(eq + 1) * 512],
                        vob[bi2][:], s_io, wait=[(s_ac, vve)])
            P.barrier([s_dv, s_ac, s_pe, s_io, s_w, s_sc])

        if dump == "kv":
            vtmp = cv(Ap, 0, 2048, BF16)
            vf32 = cv(Ap, 4 * 1024, 2048, F32)
            for g in range(NG_D):
                v0 = P.dma("gpsimd", vtmp[:], kTd[g * 128:(g + 1) * 128, :], s_io)

                def fc(e):
                    return e.tensor_copy(out=vf32[:], in_=vtmp[:])
                vv = P.op("vector", fc, sem=s_dv, wait=[(s_io, v0)])
                P.dma("gpsimd", dbg[g * 128:(g + 1) * 128, :], vf32[:], s_io,
                      wait=[(s_dv, vv)])
                v0 = P.dma("gpsimd", vtmp[:],
                           vd.rearrange("(a p) e -> p (a e)", p=128)[:, g * 2048:0]
                           if False else
                           vd[0:L, g * 128:(g + 1) * 128].rearrange("(a p) e -> p (a e)", p=128),
                           s_io, wait=[(s_dv, vv)])

                def fc2(e):
                    return e.tensor_copy(out=vf32[:], in_=vtmp[:])
                vv = P.op("vector", fc2, sem=s_dv, wait=[(s_io, v0)])
                P.dma("gpsimd", dbg[D + g * 128:D + (g + 1) * 128, :], vf32[:], s_io,
                      wait=[(s_dv, vv)])
            P.waits("gpsimd", [(s_io, s_io.n), (s_w, s_w.n)])
            _emit(nc, P, ctx)
            return nc

        att = cv(Cp, 0, 32 * 1024, BF16)
        khb = [cv(Ap, 0, 2048, BF16), cv(Ap, 4 * 1024, 2048, BF16)]
        vhb = [cv(Ap, 8 * 1024, 2048, BF16), cv(Ap, 12 * 1024, 2048, BF16)]
        scb = [cv(Ap, 16 * 1024, 2048, F32), cv(Ap, 24 * 1024, 2048, F32)]
        pbb = [cv(Ap, 32 * 1024, 2048, BF16), cv(Ap, 36 * 1024, 2048, BF16)]
        ptb = [cv(Ap, 40 * 1024, 2048, BF16), cv(Ap, 44 * 1024, 2048, BF16)]
        nm_t = [sb("nm0", [128, 1], F32), sb("nm1", [128, 1], F32)]
        sum_t = [sb("sum0", [128, 1], F32), sb("sum1", [128, 1], F32)]
        rc_t = [sb("rc0", [128, 1], F32), sb("rc1", [128, 1], F32)]
        ptps = PSB[:].bitcast(BF16)
        avv = PSB[:, 1536:1664]

        P.phase(PH_ATT)
        items = [(h, j) for h in range(H) for j in range(NBLK)]
        kh_v = [0, 0]
        vh_v = [0, 0]
        g_kh = [0, 0]
        g_vh = [0, 0]
        g_exp = [0, 0]
        g_tr = [0, 0]
        g_av = [0, 0]
        g_psa_a = 0
        g_psa_d = 0
        g_ptps = 0
        g_ae = 0

        kh_v[0] = P.dma("sync", khb[0][:], kTd[0:128, :], s_w)
        vh_v[0] = P.dma("sync", vhb[0][:].rearrange("p (a e) -> p a e", e=128),
                        vd.rearrange("(a p) e -> p a e", p=128)[:, :, 0:128], s_w)

        for idx, (h, j) in enumerate(items):
            ib = idx % 2
            hb = h % 2
            kext = (2 * j + 2) * 128
            nk = kext // 128
            if j == 0 and h + 1 < H:
                nb = (h + 1) % 2
                kh_v[nb] = P.dma("sync", khb[nb][:],
                                 kTd[(h + 1) * 128:(h + 2) * 128, :], s_w,
                                 wait=[(s_pe, g_kh[nb])])
                vh_v[nb] = P.dma("sync", vhb[nb][:].rearrange("p (a e) -> p a e", e=128),
                                 vd.rearrange("(a p) e -> p a e", p=128)[:, :,
                                     (h + 1) * 128:(h + 2) * 128], s_w,
                                 wait=[(s_pe, g_vh[nb])])

            def fsc_mm(e, h=h, j=j, hb=hb, kext=kext):
                last = None
                for c in range((kext + 511) // 512):
                    w = min(512, kext - c * 512)
                    last = e.matmul(PSA[:, c * 512:c * 512 + w],
                                    qres[:, h * 1024 + j * 128:h * 1024 + (j + 1) * 128],
                                    khb[hb][:, c * 512:c * 512 + w],
                                    start=True, stop=True)
                return last
            vscm = P.op("tensor", fsc_mm, sem=s_pe,
                        wait=[(s_w, kh_v[hb]), (s_ac, g_psa_a), (s_dv, g_psa_d)])
            g_kh[hb] = vscm

            if kext > 256:
                def fse(e, ib=ib, kext=kext):
                    return e.activation(out=scb[ib][:, 0:kext - 256],
                                        in_=PSA[:, 0:kext - 256], func=AF.Copy)
                vse = P.op("scalar", fse, sem=s_ac, wait=[(s_pe, vscm)])
            else:
                vse = s_ac.n
            g_psa_a = vse

            def fsm(e, ib=ib, j=j, kext=kext):
                return e.scalar_tensor_tensor(out=scb[ib][:, kext - 256:kext],
                                              in0=masks[:, j * 256:(j + 1) * 256],
                                              scalar=NEG, in1=PSA[:, kext - 256:kext],
                                              op0=ALU.mult, op1=ALU.add)
            vsm = P.op("vector", fsm, sem=s_dv,
                       wait=[(s_pe, vscm), (s_ac, g_exp[ib])])
            g_psa_d = vsm

            def fmx(e, ib=ib, kext=kext):
                return e.tensor_reduce(out=nm_t[ib][:], in_=scb[ib][:, 0:kext],
                                       axis=mybir.AxisListType.X, op=ALU.max, negate=True)
            vmx = P.op("vector", fmx, sem=s_dv, wait=[(s_ac, vse), (s_ac, g_exp[ib])])

            def fex(e, ib=ib, kext=kext):
                return e.activation(out=pbb[ib][:, 0:kext], in_=scb[ib][:, 0:kext],
                                    func=AF.Exp, bias=nm_t[ib][:, 0:1],
                                    accum_out=sum_t[ib][:, 0:1])
            vex = P.op("scalar", fex, sem=s_ac, wait=[(s_dv, vmx), (s_pe, g_tr[ib])])
            g_exp[ib] = vex

            def frc(e, ib=ib):
                return e.reciprocal(out=rc_t[ib][:], in_=sum_t[ib][:])
            P.op("vector", frc, sem=s_dv, wait=[(s_ac, vex)])

            def fnr(e, ib=ib, kext=kext):
                return e.tensor_scalar(out=pbb[ib][:, 0:kext], in0=pbb[ib][:, 0:kext],
                                       scalar1=rc_t[ib][:, 0:1], scalar2=None, op0=ALU.mult)
            vnr = P.op("vector", fnr, sem=s_dv)

            vpt_e = 0
            for half in range((nk + 7) // 8):
                nkk = min(8, nk - half * 8)

                def ftr2(e, ib=ib, half=half, nkk=nkk):
                    last = None
                    for kb in range(nkk):
                        last = e.transpose(ptps[:, kb * 128:(kb + 1) * 128],
                                           pbb[ib][:, (half * 8 + kb) * 128:
                                                   (half * 8 + kb + 1) * 128], ident_b[:])
                    return last
                vtr2 = P.op("tensor", ftr2, sem=s_pe,
                            wait=[(s_dv, vnr), (s_ac, g_ptps)])

                def fpe(e, ib=ib, half=half, nkk=nkk):
                    return e.activation(out=ptb[ib][:, half * 1024:half * 1024 + nkk * 128],
                                        in_=ptps[:, 0:nkk * 128], func=AF.Copy)
                vpt_e = P.op("scalar", fpe, sem=s_ac,
                             wait=[(s_pe, vtr2), (s_pe, g_av[ib])])
                g_ptps = vpt_e
            g_tr[ib] = s_pe.n

            def fav(e, ib=ib, hb=hb, nk=nk):
                last = None
                for kb in range(nk):
                    last = e.matmul(avv,
                                    r3(vhb[hb][:], 128)[:, kb, :],
                                    ptb[ib][:, kb * 128:(kb + 1) * 128],
                                    start=(kb == 0), stop=(kb == nk - 1))
                return last
            vav = P.op("tensor", fav, sem=s_pe,
                       wait=[(s_ac, vpt_e), (s_w, vh_v[hb]), (s_ac, g_ae)])
            g_vh[hb] = vav
            g_av[ib] = vav

            def fae(e, h=h, j=j):
                return e.activation(out=att[:, h * 1024 + j * 128:h * 1024 + (j + 1) * 128],
                                    in_=avv, func=AF.Identity, bias=bv_t[:, h:h + 1],
                                    scale=1.0)
            g_ae = P.op("scalar", fae, sem=s_ac, wait=[(s_pe, vav)])

        P.barrier([s_dv, s_ac, s_pe, s_io, s_w, s_sc])

        P.phase(PH_OPROJ)
        wof = cv(Ap, 0, 32 * 128, F32)
        wob = [cv(Ap, 16 * 1024, 32 * 128, BF16), cv(Ap, 24 * 1024, 32 * 128, BF16)]
        oev = [cv(Ap, 32 * 1024, 512, F32), cv(Ap, 34 * 1024, 512, F32)]
        xrr = [cv(Ap, 36 * 1024, 512, F32), cv(Ap, 38 * 1024, 512, F32)]
        xo2 = [cv(Ap, 40 * 1024, 512, F32), cv(Ap, 42 * 1024, 512, F32)]
        io_o = [0, 0]
        dv_w = [0, 0]
        for fc in range(32):
            bi = fc % 2
            vwf = P.dma("sync", wof[:], woi[:, fc], s_w, wait=[(s_dv, s_dv.n)])

            def fcw(e, bi=bi):
                return e.tensor_copy(out=wob[bi][:], in_=wof[:])
            vcw = P.op("vector", fcw, sem=s_dv, wait=[(s_w, vwf), (s_pe, s_pe.n)])
            dv_w[bi] = vcw

            for hf in range(2):
                def fom(e, bi=bi, hf=hf):
                    last = None
                    for ge in range(32):
                        last = e.matmul(a_[hf], wob[bi][:, ge * 128:(ge + 1) * 128],
                                        att[:, ge * 1024 + hf * 512:ge * 1024 + (hf + 1) * 512],
                                        start=(ge == 0), stop=(ge == 31))
                    return last
                vom = P.op("tensor", fom, sem=s_pe, wait=[(s_dv, vcw), (s_ac, s_ac.n)])

                def foe(e, fc=fc, hf=hf):
                    return e.activation(out=oev[hf][:], in_=a_[hf], func=AF.Identity,
                                        bias=bo_t[:, fc:fc + 1], scale=1.0)
                voe = P.op("scalar", foe, sem=s_ac, wait=[(s_pe, vom), (s_dv, s_dv.n)])

                xv = r3(xmid[fc * 128:(fc + 1) * 128, hf * 1024:(hf + 1) * 1024], 128)
                vx = P.dma("gpsimd", r3(xrr[hf][:], 128), xv[:, 0:8:2, :], s_io,
                           wait=[(s_dv, s_dv.n)])

                def fxa(e, hf=hf):
                    return e.tensor_tensor(out=xo2[hf][:], in0=oev[hf][:], in1=xrr[hf][:],
                                           op=ALU.add)
                vxa = P.op("vector", fxa, sem=s_dv,
                           wait=[(s_ac, voe), (s_io, vx), (s_io, io_o[hf])])
                vxa2 = fence_v()
                io_o[hf] = P.dma("gpsimd",
                                 xmid2[fc * 128:(fc + 1) * 128, hf * 512:(hf + 1) * 512],
                                 xo2[hf][:], s_io, wait=[(s_dv, vxa2)])
        P.barrier([s_dv, s_ac, s_pe, s_io, s_w, s_sc])

        if dump == "xmid2":
            vtmp2 = cv(Ap, 0, 1024, F32)
            for g in range(NG_D):
                v0 = P.dma("gpsimd", vtmp2[:], xmid2[g * 128:(g + 1) * 128, :], s_io)
                P.dma("gpsimd", dbg[g * 128:(g + 1) * 128, :], vtmp2[:], s_io,
                      wait=[(s_io, v0)])
            P.waits("gpsimd", [(s_io, s_io.n), (s_w, s_w.n)])
            _emit(nc, P, ctx)
            return nc

        for t in range(2):
            ln_phase(xmid2, t * 512, lnsl, xg_out=xg_t[0:NG_D, :])
            P.barrier([s_dv, s_ac, s_pe, s_io])
            gate_up(first_tile=False)
            P.barrier([s_dv, s_ac, s_pe, s_w, s_io])
            down(False, xmid2, t * 512, oT, t * 512)
            P.barrier([s_dv, s_ac, s_pe, s_w, s_io, s_sc, s_gp])

        P.waits("gpsimd", [(s_io, s_io.n), (s_w, s_w.n), (s_sc, s_sc.n)])
        _emit(nc, P, ctx)
    return nc


def _emit(nc, P, ctx):
    block = ctx.enter_context(nc.Block())
    for name in ENGINES:
        deco = getattr(block, name)

        def body(e, ops=P.ops[name]):
            for f in ops:
                f(e)
        deco(body)


def _stage_gu(qw):
    by = np.ascontiguousarray(qw).view(np.uint8).reshape(D, I // 2)[PI_D]
    return np.ascontiguousarray(by.reshape(NG_D, 128, NS2, 128).transpose(1, 2, 0, 3))


def _stage_dq(qw):
    by = np.ascontiguousarray(qw).view(np.uint8).reshape(I, D // 2)[PI_I]
    return np.ascontiguousarray(by.reshape(NG_I, 128, D // 2).transpose(1, 0, 2))


def _stage_wT(W, perm_rows, perm_cols):
    wT = np.asarray(W).T
    if perm_rows:
        wT = wT[PI_D]
    if perm_cols:
        wT = wT[:, PI_D]
    return np.ascontiguousarray(
        wT.reshape(32, 128, 32, 128).transpose(1, 2, 0, 3), dtype=np.float32)


def _stage_wv(W):
    wT = np.asarray(W).T[PI_D]
    return np.ascontiguousarray(wT.reshape(NG_D, 128, D).transpose(1, 0, 2),
                                dtype=np.float32)


def _stage_b(b, perm):
    bb = np.asarray(b)[PI_D] if perm else np.asarray(b)
    return np.ascontiguousarray(bb.reshape(32, 128).T, dtype=np.float32)


def _qz_bytes(qz):
    return np.ascontiguousarray(np.ascontiguousarray(qz).view(np.uint8).reshape(
        qz.shape[0], -1))


def stage_core(c, x, w):
    b, p = c // 2, c % 2
    blocks = np.arange(16)
    if p:
        blocks = blocks.reshape(8, 2)[:, ::-1].reshape(-1)
    cols = (blocks[:, None] * 128 + np.arange(128)[None, :]).reshape(-1)
    m = {
        "xT": np.ascontiguousarray(np.asarray(x[b]).T[PI_D][:, cols], dtype=np.float32),
        "qpos": np.ascontiguousarray(
            cols.reshape(16, 128)[0::2].T, dtype=np.float32),
        "kpos": np.ascontiguousarray(cols[None, :], dtype=np.float32),
    }
    m.update(w)
    return m, cols


def kernel(**inp):
    x = np.asarray(inp["L0_attn_x"], dtype=np.float32)
    w = {
        "qg": _stage_gu(inp["L0_ffn_gate_qweight"]),
        "qu": _stage_gu(inp["L0_ffn_up_qweight"]),
        "sg": np.ascontiguousarray(np.asarray(inp["L0_ffn_gate_scales"],
                                              dtype=np.float32)[:, PI_I]),
        "su": np.ascontiguousarray(np.asarray(inp["L0_ffn_up_scales"],
                                              dtype=np.float32)[:, PI_I]),
        "zgq": _qz_bytes(inp["L0_ffn_gate_qzeros"]),
        "zuq": _qz_bytes(inp["L0_ffn_up_qzeros"]),
        "qd": _stage_dq(inp["L0_ffn_down_qweight"]),
        "sd": np.ascontiguousarray(np.asarray(inp["L0_ffn_down_scales"],
                                              dtype=np.float32)[:, PI_D]),
        "zdq": _qz_bytes(inp["L0_ffn_down_qzeros"]),
        "wq": _stage_wT(inp["L0_attn_Wq"], True, False),
        "wk": _stage_wT(inp["L0_attn_Wk"], True, False),
        "wo": _stage_wT(inp["L0_attn_Wo"].T if False else inp["L0_attn_Wo"],
                        False, True),
        "wv": _stage_wv(inp["L0_attn_Wv"]),
        "bq": _stage_b(inp["L0_attn_bq"], False),
        "bk": _stage_b(inp["L0_attn_bk"], False),
        "bv": _stage_b(inp["L0_attn_bv"], False),
        "bo": _stage_b(inp["L0_attn_bo"], True),
    }
    nc = build_program()
    in_maps = []
    colmaps = []
    for c in range(8):
        m, cols = stage_core(c, x, w)
        in_maps.append(m)
        colmaps.append(cols)
    res = run_bass_kernel_spmd(nc, in_maps, core_ids=list(range(8)))

    out = np.empty((4, L, D), dtype=np.float32)
    inv = np.empty(D, dtype=np.int64)
    inv[PI_D] = np.arange(D)
    for c in range(8):
        b = c // 2
        own = colmaps[c].reshape(16, 128)[0::2].reshape(-1)
        oTc = np.asarray(res.results[c]["oT"])
        out[b, own, :] = oTc[inv].T
    return out
